# version 02
# baseline (speedup 1.0000x reference)
import numpy as np
import ml_dtypes
from contextlib import ExitStack
from dataclasses import dataclass, field
import concourse.bass as bass
import concourse.mybir as mybir
from concourse.bass_utils import run_bass_kernel_spmd


ENGINES = ("tensor", "vector", "scalar", "gpsimd", "sync")


@dataclass
class Op:
    engine: str
    emit: object
    after: list = field(default_factory=list)
    is_dma: bool = False
    idx: int = 0
    needs_inc: bool = False
    inc_count: int = 0
    dma_idx: int = 0
    tag: str = ""


class Token:
    __slots__ = ("op",)
    def __init__(self, op):
        self.op = op


class Flow:
    def __init__(self, nc, stack, name=""):
        self.nc = nc
        self.ops = {e: [] for e in ENGINES}
        self.n = 0
        self.name = name
        self.NSEM = 8
        self.pc_sem = {e: stack.enter_context(nc.semaphore(f"{name}pc_{e}")) for e in ENGINES}
        self.dma_sem = {e: [stack.enter_context(nc.semaphore(f"{name}dma_{e}{i}")) for i in range(self.NSEM)]
                        for e in ("sync", "gpsimd", "scalar", "vector", "tensor")}
        self.dma_n = {e: 0 for e in ("sync", "gpsimd", "scalar", "vector", "tensor")}

    def op(self, engine, emit, after=(), tag=""):
        o = Op(engine, emit, [t.op for t in after if t is not None], False, self.n, tag=tag)
        self.ops[engine].append(o)
        self.n += 1
        return Token(o)

    def dma(self, engine, emit, after=(), tag=""):
        o = Op(engine, emit, [t.op for t in after if t is not None], True, self.n, tag=tag)
        o.dma_idx = self.dma_n[engine]
        self.dma_n[engine] += 1
        self.ops[engine].append(o)
        self.n += 1
        return Token(o)

    def emit(self, blk):
        for e in ENGINES:
            for o in self.ops[e]:
                for d in o.after:
                    d.needs_inc = True
        for e in ENGINES:
            c_pc = 0
            for o in self.ops[e]:
                if o.is_dma:
                    o.inc_count = 16 * (o.dma_idx // self.NSEM + 1)
                    o.needs_inc = True
                elif o.needs_inc:
                    c_pc += 1
                    o.inc_count = c_pc
        done = set()
        ptr = {e: 0 for e in ENGINES}
        progressed = True
        total = self.n
        while len(done) < total:
            progressed = False
            for e in ENGINES:
                while ptr[e] < len(self.ops[e]):
                    o = self.ops[e][ptr[e]]
                    if all((d.engine == e and not d.is_dma and d.idx < o.idx) or (id(d) in done)
                           for d in o.after):
                        done.add(id(o))
                        ptr[e] += 1
                        progressed = True
                    else:
                        break
            if not progressed:
                stuck = [(e, self.ops[e][ptr[e]].tag, self.ops[e][ptr[e]].idx)
                         for e in ENGINES if ptr[e] < len(self.ops[e])]
                raise RuntimeError(f"Flow deadlock; stuck at {stuck}")
        def mk_section(e):
            def body(eng):
                for o in self.ops[e]:
                    waits = {}
                    for d in o.after:
                        sem = self.dma_sem[d.engine][d.dma_idx % self.NSEM] if d.is_dma else self.pc_sem[d.engine]
                        key = id(sem)
                        cur = waits.get(key)
                        if cur is None or cur[1] < d.inc_count:
                            waits[key] = (sem, d.inc_count)
                    if o.is_dma and o.dma_idx >= self.NSEM:
                        sem = self.dma_sem[e][o.dma_idx % self.NSEM]
                        key = id(sem)
                        v = 16 * (o.dma_idx // self.NSEM)
                        cur = waits.get(key)
                        if cur is None or cur[1] < v:
                            waits[key] = (sem, v)
                    for sem, v in waits.values():
                        eng.wait_ge(sem, v)
                    inst = o.emit(eng)
                    if o.is_dma:
                        inst.then_inc(self.dma_sem[e][o.dma_idx % self.NSEM], 16)
                    elif o.needs_inc:
                        inst.then_inc(self.pc_sem[e], 1)
            return body

        for e in ENGINES:
            if not self.ops[e]:
                continue
            body = mk_section(e)
            getattr(blk, e)(body)


F32 = mybir.dt.float32
BF16 = mybir.dt.bfloat16
AF = mybir.ActivationFunctionType
ALU = mybir.AluOpType
AX = mybir.AxisListType
BF = ml_dtypes.bfloat16
THR = 0.5009765625


def tok_idx(r):
    i = r % 4
    return np.concatenate([np.arange(512 * i, 512 * (i + 1)),
                           np.arange(512 * (7 - i), 512 * (8 - i))])


def emit_quant(f, nc, sb, w_sl, ntiles, width, ngroups, out_sl, acc_col, pred, acc_tok_in, start_tok=None):
    toks = []
    acc_tok = acc_tok_in
    prev_store = start_tok
    prev_ops = [start_tok, start_tok]
    for i in range(ntiles):
        ldt = f.dma('gpsimd', lambda e, i=i: e.dma_start(out=sb['qw_bf'][:, :width], in_=w_sl(i)),
                    after=[t for t in (prev_ops[0],) if t], tag=f"q_ld{i}")
        up = f.op('vector', lambda e: e.tensor_copy(sb['qw_f'][:, :width], sb['qw_bf'][:, :width]),
                  after=[ldt, prev_ops[1]], tag="q_up")
        gs = f.op('vector', lambda e: e.tensor_reduce(
            sb['qsc'][:, :ngroups].rearrange("p n -> p n 1" if False else "p (n o) -> p n o", o=1),
            sb['qw_f'][:, :width].rearrange("p (n o) -> p n o", o=64),
            AX.X, ALU.add, apply_absolute_value=True), after=[up], tag="q_gs")
        if acc_col is not None:
            acc_tok = f.op('vector', lambda e: e.tensor_scalar(
                sb['acc1'][:, acc_col:acc_col + 1], sb['qsc'][:, :ngroups], 1.0, sb['acc1'][:, acc_col:acc_col + 1],
                ALU.mult, ALU.add) if False else e.scalar_tensor_tensor(
                sb['acc1'][:, acc_col:acc_col + 1],
                sb['qsc'][:, 0:1], 1.0, sb['acc1'][:, acc_col:acc_col + 1], ALU.mult, ALU.add),
                after=[gs, acc_tok], tag="q_acc")
        scbf = f.op('scalar', lambda e: e.mul(sb['qsc_bf'][:, :ngroups], sb['qsc'][:, :ngroups], 1.0 / 64),
                    after=[gs], tag="q_scbf")
        scf = f.op('vector', lambda e: e.tensor_copy(sb['qsc'][:, :ngroups], sb['qsc_bf'][:, :ngroups]),
                   after=[scbf], tag="q_scf")
        scf = f.op('vector', lambda e: e.tensor_scalar_max(sb['qsc'][:, :ngroups], sb['qsc'][:, :ngroups], 1e-8),
                   after=[scf], tag="q_clip")
        th = f.op('vector', lambda e: e.tensor_scalar(sb['qth'][:, :ngroups], sb['qsc'][:, :ngroups], THR, None, ALU.mult),
                  after=[scf], tag="q_th")
        absw = f.op('scalar', lambda e: e.activation(sb['qaux'][:, :width], sb['qw_f'][:, :width], AF.Abs),
                    after=[up], tag="q_abs")
        ge = f.op('vector', lambda e: e.tensor_tensor(
            sb['qaux'][:, :width].rearrange("p (n o) -> p n o", o=64),
            sb['qaux'][:, :width].rearrange("p (n o) -> p n o", o=64),
            sb['qth'][:, :ngroups].rearrange("p (n o) -> p n o", o=1).broadcast_to((128, ngroups, 64)),
            ALU.is_gt), after=[th, absw], tag="q_ge")
        sgn = f.op('scalar', lambda e: e.sign(sb['qaux2'][:, :width], sb['qw_f'][:, :width]),
                   after=[up], tag="q_sgn")
        q = f.op('vector', lambda e: e.tensor_mul(sb['qaux2'][:, :width], sb['qaux2'][:, :width], sb['qaux'][:, :width]),
                 after=[ge, sgn], tag="q_q")
        wh = f.op('vector', lambda e: e.tensor_tensor(
            sb['qaux2'][:, :width].rearrange("p (n o) -> p n o", o=64),
            sb['qaux2'][:, :width].rearrange("p (n o) -> p n o", o=64),
            sb['qsc'][:, :ngroups].rearrange("p (n o) -> p n o", o=1).broadcast_to((128, ngroups, 64)),
            ALU.mult), after=[q], tag="q_wh")
        d = f.op('vector', lambda e: e.tensor_sub(sb['qaux'][:, :width], sb['qaux2'][:, :width], sb['qw_f'][:, :width]),
                 after=[wh], tag="q_d")
        dbf = f.op('vector', lambda e: e.tensor_copy(sb['qw_bf'][:, :width], sb['qaux'][:, :width]),
                   after=[d, ldt], tag="q_dbf")
        dup = f.op('vector', lambda e: e.tensor_copy(sb['qaux'][:, :width], sb['qw_bf'][:, :width]),
                   after=[dbf], tag="q_dup")
        wq = f.op('vector', lambda e: e.tensor_add(sb['qout'][:, :width], sb['qw_f'][:, :width], sb['qaux'][:, :width]),
                  after=[dup, prev_store], tag="q_wq")
        stt = f.dma('sync', lambda e, i=i: e.dma_start(out=out_sl(i), in_=sb['qout'][:, :width]),
                    after=[wq], tag=f"q_st{i}")
        prev_store = stt
        prev_ops = [dbf, wq]
        prev_ops[0] = dup
        toks.append(stt)
    return toks[-1], acc_tok


def emit_abssum(f, nc, sb, w_sl, ntiles, width, acc_col, start_tok, war_tok=None):
    t = start_tok
    ld_prev = war_tok
    for i in range(ntiles):
        ldt = f.dma('gpsimd', lambda e, i=i: e.dma_start(out=sb['qw_f'][:, :width], in_=w_sl(i)),
                    after=[t for t in (ld_prev,) if t], tag=f"as_ld{i}")
        rd = f.op('vector', lambda e: e.tensor_reduce(
            sb['qsc'][:, 0:1], sb['qw_f'][:, :width], AX.X, ALU.add, apply_absolute_value=True),
            after=[ldt, t], tag="as_rd")
        t = f.op('vector', lambda e: e.tensor_add(sb['acc1'][:, acc_col:acc_col + 1],
                                                  sb['acc1'][:, acc_col:acc_col + 1], sb['qsc'][:, 0:1]),
                 after=[rd], tag="as_acc")
        ld_prev = rd
    return t


def build_l1():
    nc = bass.Bass("TRN2", target_bir_lowering=False, debug=False)
    dt = nc.dram_tensor
    x_rows = dt("x_rows", [1024, 2048], F32, kind="ExternalInput").ap()
    Wqkvz = dt("Wqkvz", [2048, 12288], F32, kind="ExternalInput").ap()
    Wba = dt("Wba", [2048, 64], F32, kind="ExternalInput").ap()
    alog = dt("alog", [1, 32], F32, kind="ExternalInput").ap()
    dtbt = dt("dtb", [1, 32], F32, kind="ExternalInput").ap()
    idn = dt("idn", [128, 128], BF16, kind="ExternalInput").ap()
    consts = dt("consts", [128, 8], F32, kind="ExternalInput").ap()
    wgu_sl = dt("wgu_sl", [2048, 2048], F32, kind="ExternalInput").ap()
    wp_sl = dt("wp_sl", [256, 8192], F32, kind="ExternalInput").ap()
    w1_sl = dt("w1_sl", [1024, 2048], F32, kind="ExternalInput").ap()
    w2_sl = dt("w2_sl", [256, 8192], F32, kind="ExternalInput").ap()

    xn_out = dt("xn_out", [1024, 2048], BF16, kind="ExternalOutput").ap()
    proj_out = dt("proj_out", [1024, 12288], BF16, kind="ExternalOutput").ap()
    beta_out = dt("beta_out", [1024, 32], F32, kind="ExternalOutput").ap()
    g_out = dt("g_out", [1024, 32], F32, kind="ExternalOutput").ap()
    wqgu_q = dt("wqgu_q", [2048, 2048], BF16, kind="ExternalOutput").ap()
    wqp_q = dt("wqp_q", [256, 8192], BF16, kind="ExternalOutput").ap()
    wabs = dt("wabs", [1, 2], F32, kind="ExternalOutput").ap()

    st = ExitStack()
    E = st.enter_context
    sb = {}
    def S(name, shape, d=F32):
        sb[name] = E(nc.sbuf_tensor(name, shape, d))
        return sb[name]

    xtile = [S(f"xtile{i}", [128, 2048]) for i in range(2)]
    sq = S("sq", [128, 2048])
    xn_sb = S("xn_sb", [128, 8 * 2048], BF16)
    xnT = S("xnT", [128, 16 * 1024], BF16)
    ident = S("ident", [128, 128], BF16)
    wbuf = [S(f"wbuf{i}", [128, 16 * 512], BF16) for i in range(2)]
    babuf = S("babuf", [128, 16 * 64], BF16)
    evac = [S(f"evac{i}", [128, 512], BF16) for i in range(4)]
    stats = S("stats", [128, 8])
    ba_sb = S("ba_sb", [128, 8 * 64])
    bg_sb = S("bg_sb", [128, 64])
    small = S("small", [1, 128])
    bc32 = S("bc32", [128, 64])
    onesc = S("onesc", [128, 1])
    ones_r = S("ones_r", [1, 128])
    cons = S("cons", [128, 8])
    S("qw_bf", [128, 4096], BF16)
    S("qw_f", [128, 4096])
    S("qaux", [128, 4096])
    S("qaux2", [128, 4096])
    S("qsc", [128, 64])
    S("qsc_bf", [128, 64], BF16)
    S("qth", [128, 64])
    S("qout", [128, 4096], BF16)
    S("acc1", [128, 2])
    wabs_sb = S("wabs_sb", [1, 2])

    pp = [E(nc.psum_tensor(f"pp{i}", [128, 512], F32)) for i in range(3)]
    pt = [E(nc.psum_tensor(f"pt{i}", [128, 128], BF16)) for i in range(1)]
    pb = E(nc.psum_tensor("pb", [128, 64], F32))
    ps1 = [E(nc.psum_tensor(f"ps1_{i}", [128, 64], F32)) for i in range(2)]
    pred = E(nc.psum_tensor("pred", [1, 2], F32))

    blk = E(nc.Block())
    f = Flow(nc, st, "f_")

    Wr = Wqkvz.rearrange("(a p) n -> p a n", p=128)
    Wba_r = Wba.rearrange("(a p) n -> p a n", p=128)
    x_r = x_rows.rearrange("(a p) n -> p a n", p=128)
    xn_or = xn_out.rearrange("(a p) n -> p a n", p=128)
    proj_or = proj_out.rearrange("(a p) n -> p a n", p=128)
    beta_or = beta_out.rearrange("(a p) n -> p a n", p=128)
    g_or = g_out.rearrange("(a p) n -> p a n", p=128)

    NCH = 24

    t_id = f.dma('sync', lambda e: e.dma_start(out=ident[:], in_=idn), tag="ld_id")
    t_cons = f.dma('sync', lambda e: e.dma_start(out=cons[:], in_=consts), tag="ld_cons")
    nc.const_aps.register(F32, 1e-5, cons[:, 0:1])
    nc.const_aps.register(F32, 1.0, cons[:, 1:2])
    nc.const_aps.register(F32, 1e-6, cons[:, 2:3])
    nc.const_aps.register(F32, 1.1920929e-07, cons[:, 3:4])
    t_al = f.dma('sync', lambda e: e.dma_start(out=small[0:1, 0:32], in_=alog), tag="ld_al")
    t_db = f.dma('sync', lambda e: e.dma_start(out=small[0:1, 32:64], in_=dtbt), tag="ld_db")
    t_ones = f.op('vector', lambda e: e.memset(ones_r[:], 1.0), after=[t_al, t_db], tag="ones")
    t_onesc = f.op('vector', lambda e: e.memset(onesc[:], 1.0), tag="onesc")
    t_acc0 = f.op('vector', lambda e: e.memset(sb['acc1'][:], 0.0), tag="acc0")

    xn_toks = []
    tr_toks = []
    prev_ln = [None, None]
    prev_tr_evac = {}
    for a in range(8):
        ldx = f.dma('sync', lambda e, a=a: e.dma_start(out=xtile[a % 2][:], in_=x_r[:, a, :]),
                    after=[prev_ln[a % 2]], tag=f"ldx{a}")
        sqk = f.op('scalar', lambda e, a=a: e.activation(sq[:], xtile[a % 2][:], AF.Square,
                                                         accum_out=stats[:, 2:3]), after=[ldx], tag=f"sq{a}")
        rs = f.op('vector', lambda e, a=a: e.reduce_sum(stats[:, 0:1], xtile[a % 2][:], axis=AX.X),
                  after=[ldx], tag=f"rs{a}")
        mean = f.op('vector', lambda e: e.tensor_scalar(stats[:, 1:2], stats[:, 0:1], 1.0 / 2048, None, ALU.mult),
                    after=[rs], tag="mean")
        ex2 = f.op('vector', lambda e: e.tensor_scalar(stats[:, 3:4], stats[:, 2:3], 1.0 / 2048, None, ALU.mult),
                   after=[sqk], tag="ex2")
        msq = f.op('vector', lambda e: e.tensor_mul(stats[:, 4:5], stats[:, 1:2], stats[:, 1:2]),
                   after=[mean], tag="msq")
        var = f.op('vector', lambda e: e.tensor_sub(stats[:, 5:6], stats[:, 3:4], stats[:, 4:5]),
                   after=[ex2, msq], tag="var")
        lnv = f.op('scalar', lambda e: e.activation(stats[:, 7:8], stats[:, 5:6], AF.Ln, bias=1e-5),
                   after=[var, t_cons], tag="lnv")
        rstd = f.op('scalar', lambda e: e.activation(stats[:, 6:7], stats[:, 7:8], AF.Exp, scale=-0.5),
                    after=[lnv], tag="rstd")
        xnt = f.op('vector', lambda e, a=a: e.tensor_scalar(
            xn_sb[:, a * 2048:(a + 1) * 2048], xtile[a % 2][:], stats[:, 1:2], stats[:, 6:7],
            ALU.subtract, ALU.mult), after=[rstd], tag=f"xn{a}")
        prev_ln[a % 2] = xnt
        xn_toks.append(xnt)
        for dtt in range(16):
            k = a * 16 + dtt
            trk = f.op('tensor', lambda e, a=a, dtt=dtt: e.transpose(
                pt[0][:], xn_sb[:, a * 2048 + dtt * 128: a * 2048 + (dtt + 1) * 128], ident[:]),
                after=[xnt, t_id, prev_tr_evac.get(k - 1)], tag=f"tr{k}")
            evk = f.op('scalar', lambda e, a=a, dtt=dtt, k=k: e.copy(
                xnT[:, dtt * 1024 + a * 128: dtt * 1024 + (a + 1) * 128], pt[0][:]),
                after=[trk], tag=f"trev{k}")
            prev_tr_evac[k] = evk
            tr_toks.append(evk)
    t_xn_st = f.dma('sync', lambda e: e.dma_start(
        out=xn_or, in_=xn_sb[:].rearrange("p (a n) -> p a n", n=2048)), after=xn_toks, tag="st_xn")

    t_ba_ld = f.dma('gpsimd', lambda e: e.dma_start(
        out=babuf[:].rearrange("p (a n) -> p a n", n=64), in_=Wba_r), tag="ld_ba")
    wload = {}
    prev_evacs = {}
    store_toks = []
    tr_done = tr_toks[-1]
    def emit_wload(c):
        deps = []
        if c >= 2:
            deps.append(prev_evacs[(c - 2, 7)])
        wload[c] = f.dma('gpsimd', lambda e, c=c: e.dma_start(
            out=wbuf[c % 2][:].rearrange("p (a n) -> p a n", n=512),
            in_=Wr[:, :, c * 512:(c + 1) * 512]), after=deps, tag=f"ldW{c}")
    emit_wload(0)
    emit_wload(1)
    for c in range(NCH):
        for tt in range(8):
            k = c * 8 + tt
            mm_deps = [wload[c]]
            if k == 0:
                mm_deps.append(tr_done)
            if k >= 3:
                kp = k - 3
                mm_deps.append(prev_evacs[(kp // 8, kp % 8)])
            last = None
            for dtt in range(16):
                last = f.op('tensor', lambda e, c=c, tt=tt, dtt=dtt, k=k: e.matmul(
                    pp[k % 3][:], xnT[:, dtt * 1024 + tt * 128: dtt * 1024 + (tt + 1) * 128],
                    wbuf[c % 2][:, dtt * 512:(dtt + 1) * 512],
                    start=(dtt == 0), stop=(dtt == 15)),
                    after=mm_deps if dtt == 0 else [], tag=f"mm{c}_{tt}_{dtt}")
            evd = [last]
            if k >= 4:
                evd.append(store_toks[k - 4])
            ek = f.op('scalar', lambda e, k=k, tt=tt: e.copy(evac[k % 4][:], pp[k % 3][:]),
                      after=evd, tag=f"pev{k}")
            prev_evacs[(c, tt)] = ek
            stk = f.dma('sync', lambda e, c=c, tt=tt, k=k: e.dma_start(
                out=proj_or[:, tt, c * 512:(c + 1) * 512], in_=evac[k % 4][:]), after=[ek], tag=f"stp{k}")
            store_toks.append(stk)
        if c + 2 < NCH:
            emit_wload(c + 2)

    expa = f.op('scalar', lambda e: e.activation(small[0:1, 0:32], small[0:1, 0:32], AF.Exp),
                after=[t_al, t_ones], tag="expa")
    bca = f.op('tensor', lambda e: e.matmul(pb[:, 0:32], ones_r[:], small[0:1, 0:32],
                                            start=True, stop=True), after=[expa, t_ones], tag="bca")
    bcd = f.op('tensor', lambda e: e.matmul(pb[:, 32:64], ones_r[:], small[0:1, 32:64],
                                            start=True, stop=True), after=[t_db, t_ones, bca], tag="bcd")
    nega = f.op('scalar', lambda e: e.mul(bc32[:, 0:32], pb[:, 0:32], -1.0), after=[bcd], tag="nega")
    cpd = f.op('scalar', lambda e: e.copy(bc32[:, 32:64], pb[:, 32:64]), after=[bcd], tag="cpd")

    ba_ev = {}
    prev_bg = None
    for tt in range(8):
        last = None
        mm_deps = [t_ba_ld, tr_done, bcd]
        if tt >= 2:
            mm_deps.append(ba_ev[tt - 2])
        for dtt in range(16):
            last = f.op('tensor', lambda e, tt=tt, dtt=dtt: e.matmul(
                ps1[tt % 2][:], xnT[:, dtt * 1024 + tt * 128: dtt * 1024 + (tt + 1) * 128],
                babuf[:, dtt * 64:(dtt + 1) * 64],
                start=(dtt == 0), stop=(dtt == 15)), after=mm_deps if dtt == 0 else [], tag=f"bamm{tt}_{dtt}")
        ek = f.op('vector', lambda e, tt=tt: e.tensor_copy(ba_sb[:, tt * 64:(tt + 1) * 64], ps1[tt % 2][:]),
                  after=[last], tag=f"baev{tt}")
        ba_ev[tt] = ek
    bg_toks = []
    for tt in range(8):
        bav = ba_sb[:, tt * 64:(tt + 1) * 64].rearrange("p (h c) -> p h c", c=4)
        bpart = bav[:, :, 0:2]
        apart = bav[:, :, 2:4]
        t1 = f.op('scalar', lambda e, bpart=bpart: e.activation(
            bg_sb[:, 0:32].rearrange("p (h c) -> p h c", c=2), bpart, AF.Sigmoid),
            after=[ba_ev[tt]] + ([prev_bg] if prev_bg else []), tag="sig")
        st1 = f.dma('sync', lambda e, tt=tt: e.dma_start(out=beta_or[:, tt, :], in_=bg_sb[:, 0:32]),
                    after=[t1], tag=f"stb{tt}")
        a1 = f.op('vector', lambda e, apart=apart: e.tensor_add(
            bg_sb[:, 32:64].rearrange("p (h c) -> p h c", c=2), apart, bc32[:, 32:64].rearrange("p (h c) -> p h c", c=2)),
            after=[ba_ev[tt], cpd, st1], tag="apg")
        ab = f.op('scalar', lambda e: e.activation(bg_sb[:, 0:32], bg_sb[:, 32:64], AF.Abs),
                  after=[a1, st1], tag="abs")
        e1 = f.op('scalar', lambda e: e.activation(bg_sb[:, 0:32], bg_sb[:, 0:32], AF.Exp, scale=-1.0),
                  after=[ab], tag="e1")
        l1p = f.op('scalar', lambda e: e.activation(bg_sb[:, 0:32], bg_sb[:, 0:32], AF.Ln, bias=1.0),
                   after=[e1, t_cons], tag="l1p")
        mx = f.op('vector', lambda e: e.tensor_scalar_max(bg_sb[:, 32:64], bg_sb[:, 32:64], 0.0),
                  after=[l1p], tag="mx")
        sp = f.op('vector', lambda e: e.tensor_add(bg_sb[:, 32:64], bg_sb[:, 32:64], bg_sb[:, 0:32]),
                  after=[mx], tag="sp")
        gg = f.op('vector', lambda e: e.tensor_mul(bg_sb[:, 32:64], bg_sb[:, 32:64], bc32[:, 0:32]),
                  after=[sp, nega], tag="g")
        st2 = f.dma('sync', lambda e, tt=tt: e.dma_start(out=g_or[:, tt, :], in_=bg_sb[:, 32:64]),
                    after=[gg], tag=f"stg{tt}")
        prev_bg = st2
        bg_toks.append(st2)

    wgu_r = wgu_sl.rearrange("(a p) n -> p a n", p=128)
    wp_r = wp_sl.rearrange("(a p) (c w) -> p a c w", p=128, w=4096)
    w1_r = w1_sl.rearrange("(a p) n -> p a n", p=128)
    w2_r = w2_sl.rearrange("(a p) (c w) -> p a c w", p=128, w=4096)
    wqgu_or = wqgu_q.rearrange("(a p) n -> p a n", p=128)
    wqp_or = wqp_q.rearrange("(a p) (c w) -> p a c w", p=128, w=4096)

    qs1, _ = emit_quant(f, nc, sb, lambda i: wgu_r[:, i, :], 16, 2048, 32,
                        lambda i: wqgu_or[:, i, :], None, pred, t_acc0)
    qs2, _ = emit_quant(f, nc, sb, lambda i: wp_r[:, i // 2, i % 2, :], 4, 4096, 64,
                        lambda i: wqp_or[:, i // 2, i % 2, :], None, pred, t_acc0, start_tok=qs1)
    a1t = emit_abssum(f, nc, sb, lambda i: w1_r[:, i, :], 8, 2048, 0, t_acc0, war_tok=qs2)
    a2t = emit_abssum(f, nc, sb, lambda i: w2_r[:, i // 2, i % 2, :], 4, 4096, 1, a1t, war_tok=a1t)
    pm = f.op('tensor', lambda e: e.matmul(pred[:], onesc[:], sb['acc1'][:], start=True, stop=True),
              after=[a2t, t_onesc], tag="psum_acc")
    pm2 = f.op('vector', lambda e: e.tensor_copy(wabs_sb[:], pred[:]), after=[pm], tag="cp_acc")
    stw = f.dma('sync', lambda e: e.dma_start(out=wabs, in_=wabs_sb[:]), after=[pm2], tag="st_wabs")

    f.op('gpsimd', lambda e: e.engine_nop(), after=[stw, qs1, qs2, t_xn_st] + store_toks[-4:] + bg_toks, tag="fin")

    f.emit(blk)
    st.close()
    return nc


def make_inputs_l1(ins):
    x = ins["L0_attn_x"]
    idn = np.eye(128, dtype=BF)
    maps = []
    for r in range(8):
        b, ti = r // 4, tok_idx(r)
        maps.append(dict(
            x_rows=np.ascontiguousarray(x[b, ti]),
            Wqkvz=ins["L0_attn_W_qkvz"],
            Wba=ins["L0_attn_W_ba"],
            alog=ins["L0_attn_A_log"].reshape(1, 32),
            dtb=ins["L0_attn_dt_bias"].reshape(1, 32),
            idn=idn,
            consts=np.tile(np.array([[1e-5, 1.0, 1e-6, 1.1920929e-07, 0, 0, 0, 0]], dtype=np.float32), (128, 1)),
            wgu_sl=np.ascontiguousarray(ins["L0_ffn_w_gate_up"][2048 * r:2048 * (r + 1)]),
            wp_sl=np.ascontiguousarray(ins["L0_ffn_w_proj"][256 * r:256 * (r + 1)]),
            w1_sl=np.ascontiguousarray(ins["L1_ffn_w1"][1024 * r:1024 * (r + 1)]),
            w2_sl=np.ascontiguousarray(ins["L1_ffn_w2"][256 * r:256 * (r + 1)]),
        ))
    return maps


F32 = mybir.dt.float32
BF16 = mybir.dt.bfloat16
AF = mybir.ActivationFunctionType
ALU = mybir.AluOpType
AX = mybir.AxisListType
BF = ml_dtypes.bfloat16
LN_ISQ128 = -2.4260151319598084


def tok_idx(r):
    i = r % 4
    return np.concatenate([np.arange(512 * i, 512 * (i + 1)),
                           np.arange(512 * (7 - i), 512 * (8 - i))])


def build_l2():
    nc = bass.Bass("TRN2", target_bir_lowering=False, debug=False)
    dt = nc.dram_tensor
    Mq = dt("Mq", [2, 2, 128, 4099], BF16, kind="ExternalInput").ap()
    Mk = dt("Mk", [2, 2, 128, 4099], BF16, kind="ExternalInput").ap()
    Mv = dt("Mv", [2, 4, 128, 4099], BF16, kind="ExternalInput").ap()
    cwq = dt("cwq", [2, 128, 4], F32, kind="ExternalInput").ap()
    cwk = dt("cwk", [2, 128, 4], F32, kind="ExternalInput").ap()
    cwv = dt("cwv", [4, 128, 4], F32, kind="ExternalInput").ap()
    z_fmd = dt("z_fmd", [2, 4, 128, 4096], BF16, kind="ExternalInput").ap()
    g_cm = dt("g_cm", [2, 4, 64, 64], F32, kind="ExternalInput").ap()
    beta_cm = dt("beta_cm", [2, 4, 64, 64], F32, kind="ExternalInput").ap()
    beta_rowd = dt("beta_rowd", [2, 4, 4096], BF16, kind="ExternalInput").ap()
    norm_w = dt("norm_w", [1, 128], F32, kind="ExternalInput").ap()
    idn = dt("idn", [128, 128], BF16, kind="ExternalInput").ap()
    i64b2 = dt("i64b2", [128, 64], BF16, kind="ExternalInput").ap()
    utri2 = dt("utri2", [128, 64], F32, kind="ExternalInput").ap()
    idn64f2 = dt("idn64f2", [128, 64], F32, kind="ExternalInput").ap()
    mtrilS = dt("mtrilS", [128, 64], BF16, kind="ExternalInput").ap()
    mtriuS = dt("mtriuS", [128, 64], BF16, kind="ExternalInput").ap()
    consts = dt("consts", [128, 8], F32, kind="ExternalInput").ap()
    xn_fm = dt("xn_fm", [2048, 1024], BF16, kind="ExternalInput").ap()
    wgu_t = dt("wgu_t", [128, 128, 16, 128], BF16, kind="ExternalInput").ap()
    wpT = dt("wpT", [8192, 2048], BF16, kind="ExternalInput").ap()

    og_out = dt("og_out", [2, 4, 128, 4096], BF16, kind="ExternalOutput").ap()
    ffn0_fm = dt("ffn0_fm", [2048, 1024], F32, kind="ExternalOutput").ap()
    g_scr = dt("g_scr", [2, 4096], F32, kind="Internal").ap()
    h_scr = dt("h_scr", [64, 128, 1024], BF16, kind="Internal").ap()

    st = ExitStack()
    E = st.enter_context
    def S(name, shape, d=F32):
        return E(nc.sbuf_tensor(name, shape, d))

    blk = E(nc.Block())
    f = Flow(nc, st, "f_")

    ident = S("ident", [128, 128], BF16)
    I642 = S("I642", [128, 64], BF16)
    UT2 = S("UT2", [128, 64], F32)
    IF2 = S("IF2", [128, 64], F32)
    mLS = S("mLS", [128, 64], BF16)
    mUS = S("mUS", [128, 64], BF16)
    cons = S("cons", [128, 8])
    nrmc = S("nrmc", [128, 1])
    ones1 = S("ones1", [1, 128])
    onesb = S("onesb", [1, 128], BF16)
    onescb = S("onescb", [128, 1], BF16)
    ones11 = S("ones11", [1, 1])
    cwt = S("cwt", [128, 4])

    nc.const_aps.register(F32, 1e-6, cons[:, 2:3])
    nc.const_aps.register(F32, 1.1920929e-07, cons[:, 3:4])
    nc.const_aps.register(F32, LN_ISQ128, cons[:, 4:5])
    nc.const_aps.register(F32, 0.0, cons[:, 5:6])

    mbuf = S("mbuf", [128, 4099], BF16)
    qT = S("qT", [128, 4096], BF16)
    kT = S("kT", [128, 4096], BF16)
    vT = [S(f"vT{i}", [128, 4096], BF16) for i in range(2)]
    qgT = [S(f"qgT{i}", [128, 4096], BF16) for i in range(2)]
    kbgT = [S(f"kbgT{i}", [128, 4096], BF16) for i in range(2)]
    eA = S("eA", [128, 4096], BF16)
    eB = S("eB", [128, 4096], BF16)
    eC = S("eC", [128, 4096], BF16)
    bscr = S("bscr", [128, 4096])
    SCRA = S("SCRA", [128, 4096], BF16)
    SCRC = S("SCRC", [128, 4096], BF16)
    o_fm = [S(f"o_fm{i}", [128, 4096], BF16) for i in range(2)]
    BBT = S("BBT", [128, 6 * 128], BF16)
    negMT = S("negMT", [128, 64], BF16)
    Xp = [S(f"Xp{i}", [128, 128], BF16) for i in range(2)]
    kTok2 = S("kTok2", [128, 128], BF16)
    vTok2 = S("vTok2", [128, 128], BF16)
    kd2 = S("kd2", [128, 128], BF16)
    rr2 = S("rr2", [128, 128], BF16)
    Ssb = [S(f"Ssb{i}", [128, 128]) for i in range(2)]
    Sbf = [S(f"Sbf{i}", [128, 128], BF16) for i in range(2)]
    att2 = S("att2", [128, 64], BF16)
    gcols = S("gcols", [128, 64])
    graw = S("graw", [128, 64])
    bcols = S("bcols", [128, 64])
    w1cols = S("w1cols", [128, 64])
    glb2 = S("glb2", [128, 64])
    grow = S("grow", [1, 4096])
    berow = S("berow", [1, 4096], BF16)
    eglrow = S("eglrow", [1, 128])
    eglb = S("eglb", [128, 128])
    rstdr = S("rstdr", [1, 4096], BF16)
    evrow = S("evrow", [1, 4096])

    csc = SCRA
    xf_parts = [bscr[:].bitcast(BF16), eA[:], eB[:]]
    def xf_slice(dtt, tg):
        col = 1024 * dtt + 512 * tg
        if col < 8192:
            return xf_parts[0][:, col:col + 512]
        col -= 8192
        if col < 4096:
            return xf_parts[1][:, col:col + 512]
        col -= 4096
        return xf_parts[2][:, col:col + 512]
    wb = [qgT[0], qgT[1]]
    hfa = kbgT[0]
    wpb = kbgT[1]
    rbc = o_fm[0][:].bitcast(F32)
    outevf = o_fm[1][:].bitcast(F32)
    rsq = evrow
    rstf = grow

    pKQ = E(nc.psum_tensor("pKQ", [128, 512], F32))
    pSQ = E(nc.psum_tensor("pSQ", [128, 512], F32))
    pTT = E(nc.psum_tensor("pTT", [128, 128], BF16))
    pAP = E(nc.psum_tensor("pAP", [128, 512], F32))
    pO = E(nc.psum_tensor("pO", [128, 512], F32))
    pS = E(nc.psum_tensor("pS", [128, 512], F32))
    pB = E(nc.psum_tensor("pB", [128, 512], F32))
    pR = E(nc.psum_tensor("pR", [1, 512], F32))

    bank = {}
    def mm(bname, emit, deps, tag):
        return f.op('tensor', emit, after=list(deps) + ([bank[bname]] if bname in bank else []), tag=tag)
    def evac(bname, eng, emit, deps, tag):
        t = f.op(eng, emit, after=list(deps), tag=tag)
        bank[bname] = t
        return t

    t0 = [
        f.dma('sync', lambda e: e.dma_start(out=ident[:], in_=idn), tag="ci"),
        f.dma('sync', lambda e: e.dma_start(out=I642[:], in_=i64b2), tag="ci2"),
        f.dma('sync', lambda e: e.dma_start(out=UT2[:], in_=utri2), tag="cu"),
        f.dma('sync', lambda e: e.dma_start(out=IF2[:], in_=idn64f2), tag="cf"),
        f.dma('sync', lambda e: e.dma_start(out=mLS[:], in_=mtrilS), tag="cm1"),
        f.dma('sync', lambda e: e.dma_start(out=mUS[:], in_=mtriuS), tag="cm2"),
        f.dma('sync', lambda e: e.dma_start(out=cons[:], in_=consts), tag="cc"),
    ]
    t_ones = f.op('vector', lambda e: e.memset(ones1[:], 1.0), tag="o1")
    t_onesb = f.op('vector', lambda e: e.memset(onesb[:], 1.0), tag="o2")
    t_onescb = f.op('vector', lambda e: e.memset(onescb[:], 1.0), tag="o3")
    t_ones11 = f.op('vector', lambda e: e.memset(ones11[:], 1.0), tag="o4")
    t0 += [t_ones, t_onesb, t_onescb, t_ones11]
    t_nw = f.dma('sync', lambda e: e.dma_start(out=evrow[0:1, 0:128], in_=norm_w), tag="cn")
    t_nm = mm('pB', lambda e: e.matmul(pB[:, 0:1], evrow[0:1, 0:128], ones11[:], start=True, stop=True),
              [t_nw, t_ones11], "nwmm")
    t_nv = evac('pB', 'vector', lambda e: e.tensor_copy(nrmc[:], pB[:, 0:1]), [t_nm], "nwev")
    t0.append(t_nv)

    def bcast(row_ap, outw, dst, deps, tag):
        last = None
        lhs = onesb if row_ap.dtype == BF16 else ones1
        lt = t_onesb if row_ap.dtype == BF16 else t_ones
        for c0 in range(0, 4096, 512):
            m = mm('pB', lambda e, c0=c0, outw=outw: e.matmul(
                pB[:outw, :], lhs[0:1, :outw], row_ap[0:1, c0:c0 + 512], start=True, stop=True),
                list(deps) + [lt] + ([last] if last else []), f"{tag}m{c0}")
            last = evac('pB', 'scalar', lambda e, c0=c0, outw=outw: e.copy(dst[:outw, c0:c0 + 512], pB[:outw, :]),
                        [m], f"{tag}e{c0}")
        return last

    def colsum_sq(src_tile, deps, tag):
        sqt = f.op('scalar', lambda e: e.activation(csc[:], src_tile[:], AF.Square), after=list(deps), tag=f"{tag}sq")
        last = None
        for c0 in range(0, 4096, 512):
            m = mm('pR', lambda e, c0=c0: e.matmul(
                pR[0:1, :], onescb[:], csc[:, c0:c0 + 512], start=True, stop=True),
                [sqt, t_onescb] + ([last] if last else []), f"{tag}m{c0}")
            last = evac('pR', 'vector', lambda e, c0=c0: e.tensor_copy(evrow[0:1, c0:c0 + 512], pR[0:1, :]),
                        [m], f"{tag}e{c0}")
        return last

    def conv_silu(m_src, cw_src, dst_tile, deps, tag):
        ldm = f.dma('sync', lambda e: e.dma_start(out=mbuf[:], in_=m_src), after=list(deps), tag=f"{tag}ld")
        ldw = f.dma('gpsimd', lambda e: e.dma_start(out=cwt[:], in_=cw_src), after=list(deps), tag=f"{tag}lw")
        c1 = f.op('vector', lambda e: e.tensor_scalar(csc[:], mbuf[:, 0:4096], cwt[:, 0:1], None, ALU.mult),
                  after=[ldm, ldw] + list(deps), tag=f"{tag}c1")
        c2 = f.op('vector', lambda e: e.scalar_tensor_tensor(csc[:], mbuf[:, 1:4097], cwt[:, 1:2], csc[:], ALU.mult, ALU.add),
                  after=[c1], tag=f"{tag}c2")
        c3 = f.op('vector', lambda e: e.scalar_tensor_tensor(csc[:], mbuf[:, 2:4098], cwt[:, 2:3], csc[:], ALU.mult, ALU.add),
                  after=[c2], tag=f"{tag}c3")
        c4 = f.op('vector', lambda e: e.scalar_tensor_tensor(csc[:], mbuf[:, 3:4099], cwt[:, 3:4], csc[:], ALU.mult, ALU.add),
                  after=[c3], tag=f"{tag}c4")
        sl = f.op('scalar', lambda e: e.activation(dst_tile[:], csc[:], AF.Silu), after=[c4], tag=f"{tag}si")
        return sl

    def l2norm(dst_tile, deps, tag, isq):
        rs = colsum_sq(dst_tile, deps, tag)
        l1 = f.op('scalar', lambda e: e.activation(evrow[0:1, :], evrow[0:1, :], AF.Ln, bias=1e-6),
                  after=[rs], tag=f"{tag}ln")
        e1 = f.op('scalar', lambda e, isq=isq: e.activation(rstdr[0:1, :], evrow[0:1, :], AF.Exp, scale=-0.5,
                                                            bias=LN_ISQ128 if isq else 0.0),
                  after=[l1], tag=f"{tag}ex")
        bc = bcast(rstdr, 128, SCRA, [e1], f"{tag}bc")
        return f.op('vector', lambda e: e.tensor_mul(dst_tile[:], dst_tile[:], SCRA[:]), after=[bc], tag=f"{tag}sc")

    og_stores = []
    pair_done = list(t0)

    for b in range(2):
        for p in range(2):
            P = f"b{b}p{p}"
            jj = [2 * p, 2 * p + 1]
            dep0 = pair_done

            sq_ = conv_silu(Mq[b, p], cwq[p], qT, dep0, f"{P}cq")
            nq = l2norm(qT, [sq_], f"{P}lq", True)
            sk_ = conv_silu(Mk[b, p], cwk[p], kT, [nq], f"{P}ck")
            nk = l2norm(kT, [sk_], f"{P}lk", False)
            vtk = nk
            v_toks = []
            for ci, j in enumerate(jj):
                vtk = conv_silu(Mv[b, j], cwv[j], vT[ci], [vtk], f"{P}cv{ci}")
                v_toks.append(vtk)

            glds = [f.dma('sync', lambda e, j=j, ci=ci, b=b: e.dma_start(out=graw[64 * ci:64 * (ci + 1), :], in_=g_cm[b, j]),
                          after=dep0, tag=f"{P}gld{ci}") for ci, j in enumerate(jj)]
            blds = [f.dma('sync', lambda e, j=j, ci=ci, b=b: e.dma_start(out=bcols[64 * ci:64 * (ci + 1), :], in_=beta_cm[b, j]),
                          after=dep0, tag=f"{P}bld{ci}") for ci, j in enumerate(jj)]
            mmg = []
            for ci in range(2):
                mg = mm('pS', lambda e, ci=ci: e.matmul(
                    pS[64 * ci:64 * (ci + 1), 0:64], UT2[64 * ci:64 * (ci + 1), :], graw[64 * ci:64 * (ci + 1), :],
                    start=True, stop=True, tile_position=(64 * ci, 64 * ci)),
                    glds + [t0[2]] + ([mmg[-1]] if mmg else []), f"{P}gmm{ci}")
                mmg.append(mg)
            gcv = evac('pS', 'vector', lambda e: e.tensor_copy(gcols[:], pS[:, 0:64]), mmg, f"{P}gcv")

            chain_prep = []
            prev_chain = []
            for ci, j in enumerate(jj):
                C = f"{P}c{ci}"
                trt = mm('pAP', lambda e, ci=ci: e.transpose(
                    pAP[0:64, 0:64], gcols[64 * ci:64 * (ci + 1), :], IF2[64 * ci:64 * (ci + 1), :],
                    tile_position=(64 * ci, 0)), [gcv], f"{C}gtr")
                evt = evac('pAP', 'vector', lambda e: e.tensor_copy(bscr[0:64, 0:64], pAP[0:64, 0:64]),
                           [trt] + prev_chain, f"{C}gte")
                st1 = f.dma('sync', lambda e, ci=ci: e.dma_start(
                    out=g_scr[ci].rearrange("(p n) -> p n", p=64), in_=bscr[0:64, 0:64]), after=[evt], tag=f"{C}gst")
                ld2 = f.dma('sync', lambda e, ci=ci: e.dma_start(
                    out=grow[:], in_=g_scr[ci].rearrange("(o n) -> o n", o=1)),
                    after=[st1] + prev_chain, tag=f"{C}grl")
                brt = f.dma('sync', lambda e, j=j, b=b: e.dma_start(
                    out=berow[:], in_=beta_rowd[b, j].rearrange("(o n) -> o n", o=1)),
                    after=dep0 + prev_chain, tag=f"{C}br")
                egl = f.op('scalar', lambda e, ci=ci: e.activation(
                    eglrow[0:1, 64 * ci:64 * (ci + 1)], grow[0:1, 63:4096:64], AF.Exp), after=[ld2], tag=f"{C}egl")
                gm = mm('pB', lambda e: e.matmul(
                    pB[0:64, 0:64], ones1[0:1, 0:64], grow[0:1, 63:4096:64], start=True, stop=True),
                    [ld2, t_ones], f"{C}glm")
                gv = evac('pB', 'vector', lambda e, ci=ci: e.tensor_sub(
                    glb2[64 * ci:64 * (ci + 1), :], pB[0:64, 0:64], gcols[64 * ci:64 * (ci + 1), :]),
                    [gm, gcv], f"{C}glv")
                w1c = f.op('scalar', lambda e, ci=ci: e.activation(
                    w1cols[64 * ci:64 * (ci + 1), :], glb2[64 * ci:64 * (ci + 1), :], AF.Exp,
                    bias=cons[64 * ci:64 * (ci + 1), 5:6]), after=[gv, t0[6]], tag=f"{C}w1")
                ba = bcast(grow, 128, bscr, [ld2, evt], f"{C}ba")
                ega = f.op('scalar', lambda e: e.activation(SCRA[:], bscr[:], AF.Exp),
                           after=[ba] + ([nq] if ci == 0 else []), tag=f"{C}eg")
                qg = f.op('vector', lambda e, ci=ci: e.tensor_mul(qgT[ci][:], qT[:], SCRA[:]),
                          after=[ega, nq], tag=f"{C}qg")
                bb = bcast(berow, 128, SCRC, [brt], f"{C}bb")
                kbs = f.op('vector', lambda e: e.tensor_mul(SCRA[:], SCRA[:], SCRC[:]), after=[qg, bb], tag=f"{C}kbs")
                kbg = f.op('vector', lambda e, ci=ci: e.tensor_mul(kbgT[ci][:], kT[:], SCRA[:]),
                           after=[kbs, nk], tag=f"{C}kbg")
                o_ = 64 * ci
                t1 = f.op('vector', lambda e, ci=ci, o_=o_: e.tensor_tensor(
                    bscr[o_:o_ + 64, :].rearrange("p (n c) -> p n c", c=64),
                    gcols[64 * ci:64 * (ci + 1), :].unsqueeze(2).broadcast_to((64, 64, 64)),
                    bscr[o_:o_ + 64, :].rearrange("p (n c) -> p n c", c=64), ALU.subtract),
                    after=[kbg, gcv], tag=f"{C}t1")
                m1 = f.op('vector', lambda e, o_=o_: e.tensor_tensor(
                    SCRA[o_:o_ + 64, :].rearrange("p (n c) -> p n c", c=64),
                    bscr[o_:o_ + 64, :].rearrange("p (n c) -> p n c", c=64),
                    mLS[o_:o_ + 64, :].unsqueeze(1).broadcast_to((64, 64, 64)), ALU.mult),
                    after=[t1, kbg], tag=f"{C}m1")
                ea1 = f.op('scalar', lambda e, ci=ci, o_=o_: e.activation(
                    eA[64 * ci:64 * (ci + 1), :], SCRA[o_:o_ + 64, :], AF.Exp,
                    bias=cons[o_:o_ + 64, 5:6]), after=[m1, t0[6]], tag=f"{C}ea1")
                ea2 = f.op('vector', lambda e, ci=ci, o_=o_: e.tensor_tensor(
                    eA[64 * ci:64 * (ci + 1), :].rearrange("p (n c) -> p n c", c=64),
                    eA[64 * ci:64 * (ci + 1), :].rearrange("p (n c) -> p n c", c=64),
                    mLS[o_:o_ + 64, :].unsqueeze(1).broadcast_to((64, 64, 64)), ALU.mult),
                    after=[ea1], tag=f"{C}ea2")
                ea3 = f.op('vector', lambda e, ci=ci: e.tensor_tensor(
                    eA[64 * ci:64 * (ci + 1), :].rearrange("p (n c) -> p n c", c=64),
                    eA[64 * ci:64 * (ci + 1), :].rearrange("p (n c) -> p n c", c=64),
                    bcols[64 * ci:64 * (ci + 1), :].unsqueeze(2).broadcast_to((64, 64, 64)), ALU.mult),
                    after=[ea2] + blds, tag=f"{C}ea3")
                m2 = f.op('vector', lambda e, o_=o_: e.tensor_tensor(
                    SCRA[o_:o_ + 64, :].rearrange("p (n c) -> p n c", c=64),
                    bscr[o_:o_ + 64, :].rearrange("p (n c) -> p n c", c=64),
                    mUS[o_:o_ + 64, :].unsqueeze(1).broadcast_to((64, 64, 64)), ALU.mult),
                    after=[ea1], tag=f"{C}m2")
                ec1 = f.op('scalar', lambda e, ci=ci, o_=o_: e.activation(
                    eC[64 * ci:64 * (ci + 1), :], SCRA[o_:o_ + 64, :], AF.Exp,
                    scale=-1.0, bias=cons[o_:o_ + 64, 5:6]), after=[m2, t0[6]], tag=f"{C}ec1")
                ec2 = f.op('vector', lambda e, ci=ci, o_=o_: e.tensor_tensor(
                    eC[64 * ci:64 * (ci + 1), :].rearrange("p (n c) -> p n c", c=64),
                    eC[64 * ci:64 * (ci + 1), :].rearrange("p (n c) -> p n c", c=64),
                    mUS[o_:o_ + 64, :].unsqueeze(1).broadcast_to((64, 64, 64)), ALU.mult),
                    after=[ec1], tag=f"{C}ec2")
                eb1 = f.op('vector', lambda e, ci=ci, o_=o_: e.tensor_mul(
                    eB[64 * ci:64 * (ci + 1), :], eC[64 * ci:64 * (ci + 1), :], SCRC[o_:o_ + 64, :]),
                    after=[ec2], tag=f"{C}eb")
                ec3 = f.op('vector', lambda e, ci=ci, o_=o_: e.tensor_tensor(
                    eC[64 * ci:64 * (ci + 1), :].rearrange("p (n c) -> p n c", c=64),
                    eC[64 * ci:64 * (ci + 1), :].rearrange("p (n c) -> p n c", c=64),
                    I642[o_:o_ + 64, :].unsqueeze(1).broadcast_to((64, 64, 64)), ALU.add),
                    after=[eb1], tag=f"{C}ec3")
                prev_chain = [ea3, ec3, w1c, egl, kbg]
                chain_prep += [ea3, ec3, w1c, egl, qg, kbg]

            egm = mm('pB', lambda e: e.matmul(pB[:, 0:128], ones1[0:1, :], eglrow[0:1, :],
                                              start=True, stop=True),
                     [t for t in chain_prep], f"{P}eglm")
            egb = evac('pB', 'vector', lambda e: e.tensor_copy(eglb[:], pB[:, 0:128]), [egm], f"{P}eglb")
            prep_all = chain_prep + v_toks + [nk, egb]

            s_toks = []
            for ci in range(2):
                z1 = f.op('vector', lambda e, ci=ci: e.memset(Ssb[ci][:], 0.0), after=dep0, tag=f"{P}sz{ci}")
                z2 = f.op('vector', lambda e, ci=ci: e.memset(Sbf[ci][:], 0.0), after=[z1], tag=f"{P}szb{ci}")
                s_toks.append(z2)

            S_tok = s_toks
            o_toks = []
            prev_build = None
            prev_scan = None
            for n in range(64):
                N = f"{P}n{n}"
                sl = slice(64 * n, 64 * (n + 1))
                kp0 = mm('pKQ', lambda e, sl=sl: e.matmul(pKQ[0:64, 0:64], kT[:, sl], kT[:, sl],
                                                          start=True, stop=True),
                         prep_all if n == 0 else [prev_build], f"{N}kp0")
                kp1 = mm('pKQ', lambda e, sl=sl: e.matmul(pKQ[64:128, 0:64], kT[:, sl], kT[:, sl],
                                                          start=True, stop=True, tile_position=(0, 64)), [kp0], f"{N}kp1")
                qp0 = mm('pKQ', lambda e, sl=sl: e.matmul(pKQ[0:64, 64:128], kT[:, sl], qT[:, sl],
                                                          start=True, stop=True), [kp1], f"{N}qp0")
                qp1 = mm('pKQ', lambda e, sl=sl: e.matmul(pKQ[64:128, 64:128], kT[:, sl], qT[:, sl],
                                                          start=True, stop=True, tile_position=(0, 64)), [qp0], f"{N}qp1")
                bm = f.op('vector', lambda e, sl=sl: e.tensor_mul(BBT[:, 0:64], eA[:, sl], pKQ[:, 0:64]),
                          after=[qp1], tag=f"{N}bm")
                bmt = f.op('vector', lambda e, sl=sl: e.tensor_mul(BBT[:, 64:128], eB[:, sl], pKQ[:, 0:64]),
                           after=[bm], tag=f"{N}bmt")
                bneg = f.op('vector', lambda e: e.tensor_scalar(negMT[:], BBT[:, 64:128], -1.0, None, ALU.mult),
                            after=[bmt], tag=f"{N}bneg")
                attb = evac('pKQ', 'vector', lambda e, sl=sl: e.tensor_mul(att2[:], eC[:, sl], pKQ[:, 64:128]),
                            [bneg], f"{N}att")
                sq_ev = bmt
                for lv in range(1, 6):
                    mms = []
                    for ci in range(2):
                        r0 = 64 * ci
                        m1_ = mm('pSQ', lambda e, lv=lv, r0=r0: e.matmul(
                            pSQ[r0:r0 + 64, 0:64],
                            BBT[r0:r0 + 64, (lv - 1) * 128 + 64:(lv - 1) * 128 + 128],
                            BBT[r0:r0 + 64, (lv - 1) * 128:(lv - 1) * 128 + 64],
                            start=True, stop=True, tile_position=(r0, r0)),
                            [sq_ev] if ci == 0 else [mms[-1]], f"{N}sqa{lv}_{ci}")
                        m2_ = mm('pSQ', lambda e, lv=lv, r0=r0: e.matmul(
                            pSQ[r0:r0 + 64, 64:128],
                            BBT[r0:r0 + 64, (lv - 1) * 128:(lv - 1) * 128 + 64],
                            BBT[r0:r0 + 64, (lv - 1) * 128 + 64:(lv - 1) * 128 + 128],
                            start=True, stop=True, tile_position=(r0, r0)),
                            [m1_], f"{N}sqb{lv}_{ci}")
                        mms += [m1_, m2_]
                    sq_ev = evac('pSQ', 'scalar', lambda e, lv=lv: e.copy(BBT[:, lv * 128:(lv + 1) * 128], pSQ[:, 0:128]),
                                 [mms[-1]], f"{N}sqe{lv}")
                tv0 = mm('pTT', lambda e, sl=sl: e.transpose(pTT[0:64, :], vT[0][:, sl], ident[:]),
                         (v_toks + [prev_scan] if prev_scan else v_toks), f"{N}tv0")
                tv1 = mm('pTT', lambda e, sl=sl: e.transpose(pTT[64:128, :], vT[1][:, sl], ident[:],
                                                             tile_position=(0, 64)), [tv0], f"{N}tv1")
                vte = evac('pTT', 'scalar', lambda e: e.copy(vTok2[:], pTT[:]), [tv1], f"{N}vte")
                tk0 = mm('pTT', lambda e, sl=sl: e.transpose(pTT[0:64, :], kT[:, sl], ident[:]), [vte], f"{N}tk0")
                tk1 = mm('pTT', lambda e, sl=sl: e.transpose(pTT[64:128, :], kT[:, sl], ident[:],
                                                             tile_position=(0, 64)), [tk0], f"{N}tk1")
                kte = evac('pTT', 'scalar', lambda e: e.copy(kTok2[:], pTT[:]), [tk1], f"{N}kte")
                kdt = f.op('vector', lambda e, n=n: e.tensor_scalar(kd2[:], kTok2[:], w1cols[:, n:n + 1], None, ALU.mult),
                           after=[kte], tag=f"{N}kd")
                rm0 = mm('pAP', lambda e, sl=sl: e.matmul(pAP[0:64, 0:128], kbgT[0][:, sl], Sbf[0][:],
                                                          start=True, stop=True), [S_tok[0], S_tok[1], sq_ev], f"{N}rm0")
                rm1 = mm('pAP', lambda e, sl=sl: e.matmul(pAP[64:128, 0:128], kbgT[1][:, sl], Sbf[1][:],
                                                          start=True, stop=True, tile_position=(0, 64)), [rm0], f"{N}rm1")
                rrt = evac('pAP', 'vector', lambda e, n=n: e.scalar_tensor_tensor(
                    rr2[:], vTok2[:], bcols[:, n:n + 1], pAP[:, 0:128], ALU.mult, ALU.subtract),
                    [rm1, vte], f"{N}rr")
                xsrc = rr2
                xcur = rrt
                for ai, lv in enumerate([5, 4, 3, 2, 1, 0]):
                    xdst = Xp[ai % 2]
                    lmm = []
                    for ci in range(2):
                        r0 = 64 * ci
                        lhs = (negMT[r0:r0 + 64, :] if lv == 0 else
                               BBT[r0:r0 + 64, lv * 128 + 64:lv * 128 + 128])
                        a1 = mm('pAP', lambda e, lhs=lhs, r0=r0, xsrc=xsrc: e.matmul(
                            pAP[r0:r0 + 64, 0:128], lhs, xsrc[r0:r0 + 64, :],
                            start=True, stop=False, tile_position=(r0, r0)),
                            [xcur] + ([bneg] if lv == 0 else []) + ([lmm[-1]] if lmm else []), f"{N}ap{lv}_{ci}")
                        a2 = mm('pAP', lambda e, r0=r0, xsrc=xsrc: e.matmul(
                            pAP[r0:r0 + 64, 0:128], I642[r0:r0 + 64, :], xsrc[r0:r0 + 64, :],
                            start=False, stop=True, tile_position=(r0, r0)),
                            [a1], f"{N}api{lv}_{ci}")
                        lmm += [a1, a2]
                    xcur = evac('pAP', 'vector', lambda e, xdst=xdst: e.tensor_copy(xdst[:], pAP[:, 0:128]),
                                [lmm[-1]], f"{N}ape{lv}")
                    xsrc = xdst
                vnew = xsrc
                vnew_tok = xcur
                om = []
                for ci in range(2):
                    c0 = 64 * ci
                    o1 = mm('pO', lambda e, ci=ci, c0=c0, sl=sl: e.matmul(
                        pO[:, c0:c0 + 64], Sbf[ci][:], qgT[ci][:, sl],
                        start=True, stop=False), [vnew_tok, S_tok[ci]] + ([om[-1]] if om else []), f"{N}o1_{ci}")
                    o2 = mm('pO', lambda e, ci=ci, c0=c0, vnew=vnew: e.matmul(
                        pO[:, c0:c0 + 64], vnew[64 * ci:64 * (ci + 1), :], att2[64 * ci:64 * (ci + 1), :],
                        start=False, stop=True, tile_position=(64 * ci, 0)), [o1, attb], f"{N}o2_{ci}")
                    om += [o1, o2]
                oe0 = f.op('scalar', lambda e, sl=sl: e.copy(o_fm[0][:, sl], pO[:, 0:64]), after=[om[-1]], tag=f"{N}oe0")
                oe1 = evac('pO', 'scalar', lambda e, sl=sl: e.copy(o_fm[1][:, sl], pO[:, 64:128]), [oe0], f"{N}oe1")
                o_toks.append(oe1)
                new_S = []
                sm = []
                for ci in range(2):
                    c0 = 128 * ci
                    s1 = mm('pS', lambda e, ci=ci, c0=c0, vnew=vnew: e.matmul(
                        pS[:, c0:c0 + 128], kd2[64 * ci:64 * (ci + 1), :], vnew[64 * ci:64 * (ci + 1), :],
                        start=True, stop=True, tile_position=(64 * ci, 0)),
                        [vnew_tok, kdt] + ([sm[-1]] if sm else []), f"{N}sm{ci}")
                    sm.append(s1)
                for ci in range(2):
                    c0 = 128 * ci
                    su = f.op('vector', lambda e, ci=ci, c0=c0, n=n: e.scalar_tensor_tensor(
                        Ssb[ci][:], Ssb[ci][:], eglb[:, 64 * ci + n:64 * ci + n + 1], pS[:, c0:c0 + 128],
                        ALU.mult, ALU.add), after=[sm[-1], S_tok[ci]], tag=f"{N}su{ci}")
                    sb_ = f.op('vector', lambda e, ci=ci: e.tensor_copy(Sbf[ci][:], Ssb[ci][:]),
                               after=[su], tag=f"{N}sb{ci}")
                    new_S.append(sb_)
                    if ci == 1:
                        bank['pS'] = sb_
                S_tok = new_S
                prev_build = attb
                prev_scan = oe1

            og_done = []
            for ci, j in enumerate(jj):
                C = f"{P}og{ci}"
                zl = f.dma('sync', lambda e, j=j, b=b: e.dma_start(out=SCRC[:], in_=z_fmd[b, j]),
                           after=[o_toks[-1]] + og_done + chain_prep, tag=f"{C}zl")
                sz = f.op('scalar', lambda e: e.activation(csc[:], SCRC[:], AF.Silu), after=[zl], tag=f"{C}sz")
                og1 = f.op('vector', lambda e, ci=ci: e.tensor_mul(o_fm[ci][:], o_fm[ci][:], csc[:]),
                           after=[sz, o_toks[-1]], tag=f"{C}og1")
                rs = colsum_sq(o_fm[ci], [og1], f"{C}rs")
                l1 = f.op('scalar', lambda e: e.activation(evrow[0:1, :], evrow[0:1, :], AF.Ln,
                                                           bias=1e-6, scale=1.0 / 128), after=[rs], tag=f"{C}ln")
                ex = f.op('scalar', lambda e: e.activation(rstdr[0:1, :], evrow[0:1, :], AF.Exp, scale=-0.5),
                          after=[l1], tag=f"{C}ex")
                bc = bcast(rstdr, 128, SCRA, [ex], f"{C}bc")
                og2 = f.op('vector', lambda e, ci=ci: e.scalar_tensor_tensor(
                    o_fm[ci][:], o_fm[ci][:], nrmc[:], SCRA[:], ALU.mult, ALU.mult),
                    after=[bc, t_nv], tag=f"{C}og2")
                so = f.dma('sync', lambda e, ci=ci, j=j, b=b: e.dma_start(out=og_out[b, j], in_=o_fm[ci][:]),
                           after=[og2], tag=f"{C}st")
                og_done.append(so)
                og_stores.append(so)
            pair_done = og_done

    fstart = list(pair_done)
    xn_r = xn_fm.rearrange("(a p) n -> p a n", p=128)
    xls = []
    xls.append(f.dma('sync', lambda e: e.dma_start(
        out=xf_parts[0][:, 0:8192].rearrange("p (a n) -> p a n", n=1024), in_=xn_r[:, 0:8, :]),
        after=fstart, tag="fx0"))
    xls.append(f.dma('sync', lambda e: e.dma_start(
        out=xf_parts[1][:, 0:4096].rearrange("p (a n) -> p a n", n=1024), in_=xn_r[:, 8:12, :]),
        after=fstart, tag="fx1"))
    xls.append(f.dma('sync', lambda e: e.dma_start(
        out=xf_parts[2][:, 0:4096].rearrange("p (a n) -> p a n", n=1024), in_=xn_r[:, 12:16, :]),
        after=fstart, tag="fx2"))
    z0 = f.op('vector', lambda e: e.memset(rsq[0:1, 0:1024], 0.0), after=fstart, tag="fz")

    wld = {}
    def wslc(ht, half):
        return wb[ht % 2][:, 2048 * half:2048 * (half + 1)]
    def emit_wld(ht, half, deps):
        ht2 = ht + 64 * half
        wld[(ht, half)] = f.dma('gpsimd', lambda e, ht2=ht2, half=half, ht=ht: e.dma_start(
            out=wslc(ht, half).rearrange("p (a n) -> p a n", n=128), in_=wgu_t[ht2]),
            after=deps, tag=f"fw{ht}_{half}")
    emit_wld(0, 0, fstart)
    emit_wld(0, 1, fstart)
    h_sts = []
    sum_tok = z0
    prev_sil = {}
    prev_hmu = {}
    last_mm = {0: None, 1: None}
    for ht in range(64):
        HT = f"fht{ht}"
        if ht + 1 < 64:
            emit_wld(ht + 1, 0, [last_mm[0]] if last_mm[0] else fstart)
            emit_wld(ht + 1, 1, [last_mm[1]] if last_mm[1] else fstart)
        for tg in range(2):
            t0_ = slice(512 * tg, 512 * (tg + 1))
            gmm = None
            for dtt in range(16):
                gmm = mm('pB', lambda e, dtt=dtt, tg=tg, ht=ht: e.matmul(
                    pB[:, :], wslc(ht, 0)[:, 128 * dtt:128 * (dtt + 1)], xf_slice(dtt, tg),
                    start=(dtt == 0), stop=(dtt == 15)),
                    ([wld[(ht, 0)]] + xls) if dtt == 0 else [], f"{HT}g{tg}_{dtt}")
            umm = None
            for dtt in range(16):
                umm = mm('pS', lambda e, dtt=dtt, tg=tg, ht=ht: e.matmul(
                    pS[:, 0:512], wslc(ht, 1)[:, 128 * dtt:128 * (dtt + 1)], xf_slice(dtt, tg),
                    start=(dtt == 0), stop=(dtt == 15)),
                    ([wld[(ht, 1)]] + xls) if dtt == 0 else [], f"{HT}u{tg}_{dtt}")
            if tg == 1:
                last_mm[0] = gmm
                last_mm[1] = umm
            sil = evac('pB', 'scalar', lambda e: e.activation(hfa[:, 0:512], pB[:], AF.Silu),
                       [gmm] + ([prev_sil.get((ht, tg - 1)) or prev_sil.get((ht - 1, 1))] if (prev_sil) else []),
                       f"{HT}si{tg}")
            hmu = evac('pS', 'vector', lambda e: e.tensor_mul(hfa[:, 512:1024], hfa[:, 0:512], pS[:, 0:512]),
                       [sil, umm] + ([prev_hmu.get((ht, tg - 1)) or prev_hmu.get((ht - 1, 1))] if prev_hmu else []),
                       f"{HT}hm{tg}")
            sq2 = f.op('scalar', lambda e: e.activation(hfa[:, 1024:1536], hfa[:, 512:1024], AF.Square),
                       after=[hmu], tag=f"{HT}sq{tg}")
            smm = mm('pR', lambda e: e.matmul(pR[0:1, :], onescb[:], hfa[:, 1024:1536],
                                              start=True, stop=True), [sq2, t_onescb], f"{HT}sm{tg}")
            sadd = evac('pR', 'vector', lambda e, t0_=t0_: e.tensor_tensor(
                rsq[0:1, t0_], rsq[0:1, t0_], pR[0:1, :], ALU.add), [smm, sum_tok], f"{HT}sa{tg}")
            sum_tok = sadd
            hstt = f.dma('sync', lambda e, ht=ht, t0_=t0_: e.dma_start(
                out=h_scr[ht, :, t0_], in_=hfa[:, 512:1024]), after=[hmu, sq2], tag=f"{HT}hst{tg}")
            h_sts.append(hstt)
            prev_sil[(ht, tg)] = hmu
            prev_hmu[(ht, tg)] = hstt
    rl = f.op('scalar', lambda e: e.activation(rsq[0:1, 0:1024], rsq[0:1, 0:1024], AF.Ln, scale=1.0 / 8192,
                                               bias=1.1920929e-07), after=[sum_tok], tag="frl")
    rx = f.op('scalar', lambda e: e.activation(rstf[0:1, 0:1024], rsq[0:1, 0:1024], AF.Exp, scale=-0.5),
              after=[rl], tag="frx")
    rb_toks = []
    for tg in range(2):
        m = mm('pB', lambda e, tg=tg: e.matmul(pB[:, :], ones1[0:1, :], rstf[0:1, 512 * tg:512 * (tg + 1)],
                                               start=True, stop=True), [rx], f"frb{tg}")
        ev = evac('pB', 'vector', lambda e, tg=tg: e.tensor_copy(rbc[:, 512 * tg:512 * (tg + 1)], pB[:, :]),
                  [m], f"frbe{tg}")
        rb_toks.append(ev)

    out_sts = []
    macc_prev = {}
    pdots = ['pB', 'pS', 'pSQ', 'pAP']
    def pdot_ap(i):
        return [pB[:, :], pS[:, 0:512], pSQ[:, :], pAP[:, :]][i]
    wpT_r = wpT.rearrange("(a p) n -> p a n", p=128)
    for dob in range(4):
        for tg in range(2):
            DB = f"fdo{dob}t{tg}"
            accs = [None] * 4
            for ht in range(64):
                war = [macc_prev[(dob, tg, ht - 2)]] if ht >= 2 else \
                      ([macc_prev[(dob, tg - 1, 62 + ht)]] if tg > 0 else
                       ([macc_prev[(dob - 1, 1, 62 + ht)]] if dob > 0 else []))
                hldt = f.dma('gpsimd', lambda e, ht=ht, tg=tg: e.dma_start(
                    out=wb[ht % 2][:, 0:512], in_=h_scr[ht, :, 512 * tg:512 * (tg + 1)]),
                    after=[h_sts[-1]] + war, tag=f"{DB}h{ht}")
                wpl = f.dma('sync', lambda e, ht=ht, dob=dob: e.dma_start(
                    out=wpb[:, 512 * (ht % 2):512 * (ht % 2) + 512],
                    in_=wpT_r[:, ht, 512 * dob:512 * (dob + 1)]),
                    after=war + (fstart if (ht < 2 and tg == 0 and dob == 0) else []), tag=f"{DB}w{ht}")
                for dot in range(4):
                    accs[dot] = mm(pdots[dot], lambda e, ht=ht, dot=dot: e.matmul(
                        pdot_ap(dot)[:, :],
                        wpb[:, 512 * (ht % 2) + 128 * dot:512 * (ht % 2) + 128 * (dot + 1)],
                        wb[ht % 2][:, 0:512],
                        start=(ht == 0), stop=(ht == 63)),
                        [hldt, wpl] + ([accs[dot]] if accs[dot] else []), f"{DB}m{ht}_{dot}")
                macc_prev[(dob, tg, ht)] = accs[3]
            for dot in range(4):
                ev = evac(pdots[dot], 'vector',
                          lambda e, dot=dot, tg=tg: e.tensor_tensor(
                              outevf[:, 0:512], pdot_ap(dot)[:, :],
                              rbc[:, 512 * tg:512 * (tg + 1)], ALU.mult),
                          [accs[dot]] + rb_toks + ([out_sts[-1]] if out_sts else []), f"{DB}ev{dot}")
                stt = f.dma('sync', lambda e, dob=dob, dot=dot, tg=tg: e.dma_start(
                    out=ffn0_fm.rearrange("(a p) n -> p a n", p=128)[:, dob * 4 + dot, 512 * tg:512 * (tg + 1)],
                    in_=outevf[:, 0:512]), after=[ev], tag=f"{DB}st{dot}")
                out_sts.append(stt)
    f.op('gpsimd', lambda e: e.engine_nop(), after=og_stores + out_sts, tag="fin")
    f.emit(blk)
    st.close()
    return nc


def make_inputs_l2(ins, l1outs):
    proj = np.zeros((2, 4096, 16, 768), dtype=BF)
    beta = np.zeros((2, 4096, 32), dtype=np.float32)
    g = np.zeros((2, 4096, 32), dtype=np.float32)
    xn = np.zeros((2, 4096, 2048), dtype=BF)
    for r in range(8):
        b, ti = r // 4, tok_idx(r)
        proj[b, ti] = l1outs[r]["proj_out"].reshape(1024, 16, 768)
        beta[b, ti] = l1outs[r]["beta_out"]
        g[b, ti] = l1outs[r]["g_out"]
        xn[b, ti] = l1outs[r]["xn_out"]
    mixed = np.ascontiguousarray(proj[..., :512])
    z = np.ascontiguousarray(proj[..., 512:]).reshape(2, 4096, 32, 128)

    def conv_rows(heads, base):
        out = np.zeros((2, len(heads), 128, 4099), dtype=BF)
        tl = np.arange(8)
        dm = np.arange(512)
        for bi in range(2):
            for ii, hp in enumerate(heads):
                c = base + 128 * hp + np.arange(128)
                h = c // 512
                jjj = c % 512
                toks = (8 * jjj[:, None] + tl[None, :])
                out[bi, ii, :, 3:] = mixed[bi][toks, h[0], :].reshape(128, 4096)
        return out

    cw = ins["L0_attn_conv_w"]
    wqgu = np.concatenate([l1outs[r]["wqgu_q"] for r in range(8)], axis=0)
    wqp = np.concatenate([l1outs[r]["wqp_q"] for r in range(8)], axis=0)
    wguT = np.ascontiguousarray(wqgu.T)
    wpT = np.ascontiguousarray(wqp.T)
    wgu_t = np.ascontiguousarray(np.transpose(wguT.reshape(16, 128, 128, 128), (2, 1, 0, 3)))
    i2 = np.zeros((128, 64), dtype=BF); i2[:64] = np.eye(64, dtype=BF); i2[64:] = np.eye(64, dtype=BF)
    u2 = np.zeros((128, 64), dtype=np.float32)
    u2[:64] = np.triu(np.ones((64, 64), dtype=np.float32)); u2[64:] = u2[:64]
    if2 = i2.astype(np.float32)
    mls = np.concatenate([np.tril(np.ones((64, 64)), -1)] * 2, axis=0).astype(BF)
    mus = np.concatenate([np.triu(np.ones((64, 64)), 1)] * 2, axis=0).astype(BF)
    consts = np.tile(np.array([[1e-5, 1.0, 1e-6, 1.1920929e-07, LN_ISQ128, 0, 0, 0]], np.float32), (128, 1))
    maps = []
    for r in range(8):
        hq = [2 * r, 2 * r + 1]
        hvs = [4 * r + i for i in range(4)]
        b, ti = r // 4, tok_idx(r)
        g_h = np.transpose(g[:, :, hvs], (0, 2, 1))
        be_h = np.transpose(beta[:, :, hvs], (0, 2, 1))
        maps.append(dict(
            Mq=conv_rows(hq, 0), Mk=conv_rows(hq, 2048), Mv=conv_rows(hvs, 4096),
            cwq=np.ascontiguousarray(np.stack([cw[128 * hp:128 * (hp + 1)] for hp in hq])),
            cwk=np.ascontiguousarray(np.stack([cw[2048 + 128 * hp:2048 + 128 * (hp + 1)] for hp in hq])),
            cwv=np.ascontiguousarray(np.stack([cw[4096 + 128 * hv:4096 + 128 * (hv + 1)] for hv in hvs])),
            z_fmd=np.ascontiguousarray(np.transpose(z[:, :, hvs, :], (0, 2, 3, 1))).astype(BF),
            g_cm=np.ascontiguousarray(g_h.reshape(2, 4, 64, 64).transpose(0, 1, 3, 2)),
            beta_cm=np.ascontiguousarray(be_h.reshape(2, 4, 64, 64).transpose(0, 1, 3, 2)),
            beta_rowd=np.ascontiguousarray(be_h).astype(BF),
            norm_w=ins["L0_attn_norm_w"].reshape(1, 128),
            idn=np.eye(128, dtype=BF), i64b2=i2, utri2=u2, idn64f2=if2,
            mtrilS=mls, mtriuS=mus, consts=consts,
            xn_fm=np.ascontiguousarray(xn[b, ti].T),
            wgu_t=wgu_t, wpT=wpT,
        ))
    return maps


F32 = mybir.dt.float32
BF16 = mybir.dt.bfloat16
AF = mybir.ActivationFunctionType
ALU = mybir.AluOpType
AX = mybir.AxisListType
BF = ml_dtypes.bfloat16


def tok_idx(r):
    i = r % 4
    return np.concatenate([np.arange(512 * i, 512 * (i + 1)),
                           np.arange(512 * (7 - i), 512 * (8 - i))])


def build_l3():
    nc = bass.Bass("TRN2", target_bir_lowering=False, debug=False)
    dt = nc.dram_tensor
    x_rows = dt("x_rows", [1024, 2048], F32, kind="ExternalInput").ap()
    ffn0_rows = dt("ffn0_rows", [1024, 2048], F32, kind="ExternalInput").ap()
    og_sl = dt("og_sl", [4096, 1024], BF16, kind="ExternalInput").ap()
    W_out = dt("W_out", [4096, 2048], F32, kind="ExternalInput").ap()
    w_q = dt("w_q", [2048, 2048], F32, kind="ExternalInput").ap()
    w_k = dt("w_k", [2048, 512], F32, kind="ExternalInput").ap()
    w_v = dt("w_v", [2048, 512], F32, kind="ExternalInput").ap()
    anw = dt("anw", [1, 2048], F32, kind="ExternalInput").ap()
    ropeA = dt("ropeA", [1024, 2048], BF16, kind="ExternalInput").ap()
    ropeB = dt("ropeB", [1024, 2048], BF16, kind="ExternalInput").ap()
    idn = dt("idn", [128, 128], BF16, kind="ExternalInput").ap()
    consts = dt("consts", [128, 8], F32, kind="ExternalInput").ap()
    w1_sl = dt("w1_sl", [1024, 2048], F32, kind="ExternalInput").ap()
    w2_sl = dt("w2_sl", [256, 8192], F32, kind="ExternalInput").ap()
    wabs8 = dt("wabs8", [1, 16], F32, kind="ExternalInput").ap()

    x1_out = dt("x1_out", [1024, 2048], F32, kind="ExternalOutput").ap()
    xn1_out = dt("xn1_out", [1024, 2048], F32, kind="ExternalOutput").ap()
    q_out = dt("q_out", [1024, 2048], BF16, kind="ExternalOutput").ap()
    k_out = dt("k_out", [1024, 512], BF16, kind="ExternalOutput").ap()
    v_out = dt("v_out", [1024, 512], BF16, kind="ExternalOutput").ap()
    w1q_sl = dt("w1q_sl", [1024, 2048], BF16, kind="ExternalOutput").ap()
    w2q_sl = dt("w2q_sl", [256, 8192], BF16, kind="ExternalOutput").ap()

    st = ExitStack()
    E = st.enter_context
    def S(name, shape, d=F32):
        return E(nc.sbuf_tensor(name, shape, d))
    blk = E(nc.Block())
    f = Flow(nc, st, "f_")

    ident = S("ident", [128, 128], BF16)
    cons = S("cons", [128, 8])
    ones1 = S("ones1", [1, 128])
    og_sb = S("og_sb", [128, 32 * 1024], BF16)
    P2 = S("P2", [128, 16384], BF16)
    P3 = S("P3", [128, 16384], BF16)
    rB_sb = S("rB_sb", [128, 16384], BF16)
    xc = S("xc", [128, 512])
    fc = S("fc", [128, 512])
    x1t = S("x1t", [128, 2048])
    xnt = S("xnt", [128, 2048])
    sq = S("sq", [128, 2048])
    stats = S("stats", [128, 8])
    anwr = S("anwr", [1, 2048])
    anwb = S("anwb", [128, 2048], BF16)
    evq = S("evq", [128, 512], BF16)
    evq2 = S("evq2", [128, 512], BF16)
    thr = S("thr", [128, 2])
    wrow = S("wrow", [1, 16])

    hbf = og_sb[:, 0:16384]
    hT = og_sb[:, 16384:32768]
    rA_sb = P2
    qwb = [P3[:, 0:8192], P3[:, 8192:16384]]
    P1f = og_sb[:].bitcast(F32)
    qw_f = P1f[:, 0:4096]
    qaux = P1f[:, 4096:8192]
    qout = P3[:, 0:4096]

    pA = E(nc.psum_tensor("pA", [128, 512], F32))
    pBk = [E(nc.psum_tensor(f"pB{i}", [128, 512], F32)) for i in range(3)]
    pT = E(nc.psum_tensor("pT", [128, 128], BF16))

    bank = {}
    def mm(bname, emit, deps, tag):
        return f.op('tensor', emit, after=list(deps) + ([bank[bname]] if bname in bank else []), tag=tag)
    def evac(bname, eng, emit, deps, tag):
        t = f.op(eng, emit, after=list(deps), tag=tag)
        bank[bname] = t
        return t

    nc.const_aps.register(F32, 1e-5, cons[:, 0:1])

    x_r = x_rows.rearrange("(a p) n -> p a n", p=128)
    f_r = ffn0_rows.rearrange("(a p) n -> p a n", p=128)
    og_r = og_sl.rearrange("(a p) n -> p a n", p=128)
    Wo_r = W_out.rearrange("(a p) n -> p a n", p=128)
    x1_or = x1_out.rearrange("(a p) n -> p a n", p=128)
    xn1_or = xn1_out.rearrange("(a p) n -> p a n", p=128)
    q_or = q_out.rearrange("(a p) n -> p a n", p=128)
    k_or = k_out.rearrange("(a p) n -> p a n", p=128)
    v_or = v_out.rearrange("(a p) n -> p a n", p=128)
    rA_r = ropeA.rearrange("(a p) n -> p a n", p=128)
    rB_r = ropeB.rearrange("(a p) n -> p a n", p=128)

    t_id = f.dma('sync', lambda e: e.dma_start(out=ident[:], in_=idn), tag="ci")
    t_cc = f.dma('sync', lambda e: e.dma_start(out=cons[:], in_=consts), tag="cc")
    t_an = f.dma('sync', lambda e: e.dma_start(out=anwr[:], in_=anw), tag="ca")
    t_og = f.dma('gpsimd', lambda e: e.dma_start(out=og_sb[:].rearrange("p (a n) -> p a n", n=1024), in_=og_r), tag="cog")
    t_ones = f.op('vector', lambda e: e.memset(ones1[:], 1.0), tag="o1")
    bc_to = []
    for c0 in range(0, 2048, 512):
        m = mm('pA', lambda e, c0=c0: e.matmul(pA[:, :], ones1[:], anwr[0:1, c0:c0 + 512], start=True, stop=True),
               [t_an, t_ones] + bc_to, f"anb{c0}")
        ev = evac('pA', 'scalar', lambda e, c0=c0: e.copy(anwb[:, c0:c0 + 512], pA[:, :]), [m], f"anbe{c0}")
        bc_to.append(ev)
    t_anb = bc_to[-1]

    x1_sts = []
    prev_wl = None
    prev_ev = {}
    for c in range(4):
        wl = f.dma('gpsimd', lambda e, c=c: e.dma_start(
            out=P2[:].rearrange("p (a n) -> p a n", n=512), in_=Wo_r[:, :, 512 * c:512 * (c + 1)]),
            after=[prev_ev.get(c - 1)], tag=f"ldWo{c}")
        for a in range(8):
            lx = f.dma('sync', lambda e, a=a, c=c: e.dma_start(out=xc[:], in_=x_r[:, a, 512 * c:512 * (c + 1)]),
                       after=[prev_ev.get((c, a - 2))], tag=f"A{c}{a}lx")
            lf = f.dma('sync', lambda e, a=a, c=c: e.dma_start(out=fc[:], in_=f_r[:, a, 512 * c:512 * (c + 1)]),
                       after=[prev_ev.get((c, a - 2))], tag=f"A{c}{a}lf")
            last = None
            for dtt in range(32):
                last = mm(f'pB{a % 3}', lambda e, a=a, dtt=dtt: e.matmul(
                    pBk[a % 3][:, :], og_sb[:, dtt * 1024 + a * 128: dtt * 1024 + (a + 1) * 128],
                    P2[:, dtt * 512:(dtt + 1) * 512],
                    start=(dtt == 0), stop=(dtt == 31)), [wl, t_og] if dtt == 0 else [], f"A{c}{a}m{dtt}")
            e1 = evac(f'pB{a % 3}', 'vector', lambda e, a=a: e.tensor_add(x1t[:, 0:512], xc[:], pBk[a % 3][:, :]),
                      [last, lx, prev_ev.get((c, a - 1))], f"A{c}{a}e1")
            e2 = f.op('vector', lambda e: e.tensor_add(x1t[:, 0:512], x1t[:, 0:512], fc[:]), after=[e1, lf], tag=f"A{c}{a}e2")
            stt = f.dma('sync', lambda e, a=a, c=c: e.dma_start(out=x1_or[:, a, 512 * c:512 * (c + 1)],
                                                               in_=x1t[:, 0:512]), after=[e2], tag=f"A{c}{a}st")
            prev_ev[(c, a)] = stt
            x1_sts.append(stt)
        prev_ev[c] = x1_sts[-1]

    xn_sts = []
    hT_toks = []
    prev_h = None
    for a in range(8):
        A = f"B{a}"
        ldx = f.dma('sync', lambda e, a=a: e.dma_start(out=x1t[:], in_=x1_or[:, a, :]),
                    after=[x1_sts[-1]] + ([prev_h] if prev_h else []), tag=f"{A}lx")
        sqk = f.op('scalar', lambda e: e.activation(sq[:], x1t[:], AF.Square, accum_out=stats[:, 2:3]),
                   after=[ldx], tag=f"{A}sq")
        rs = f.op('vector', lambda e: e.reduce_sum(stats[:, 0:1], x1t[:], axis=AX.X), after=[ldx], tag=f"{A}rs")
        mean = f.op('vector', lambda e: e.tensor_scalar(stats[:, 1:2], stats[:, 0:1], 1.0 / 2048, None, ALU.mult),
                    after=[rs], tag=f"{A}mean")
        ex2 = f.op('vector', lambda e: e.tensor_scalar(stats[:, 3:4], stats[:, 2:3], 1.0 / 2048, None, ALU.mult),
                   after=[sqk], tag=f"{A}ex2")
        msq = f.op('vector', lambda e: e.tensor_mul(stats[:, 4:5], stats[:, 1:2], stats[:, 1:2]), after=[mean], tag=f"{A}msq")
        var = f.op('vector', lambda e: e.tensor_sub(stats[:, 5:6], stats[:, 3:4], stats[:, 4:5]), after=[ex2, msq], tag=f"{A}var")
        lnv = f.op('scalar', lambda e: e.activation(stats[:, 7:8], stats[:, 5:6], AF.Ln, bias=1e-5),
                   after=[var, t_cc], tag=f"{A}lnv")
        rstd = f.op('scalar', lambda e: e.activation(stats[:, 6:7], stats[:, 7:8], AF.Exp, scale=-0.5), after=[lnv], tag=f"{A}rstd")
        xnk = f.op('vector', lambda e: e.tensor_scalar(xnt[:], x1t[:], stats[:, 1:2], stats[:, 6:7],
                                                       ALU.subtract, ALU.mult),
                   after=[rstd] + xn_sts[-1:], tag=f"{A}xn")
        stxn = f.dma('sync', lambda e, a=a: e.dma_start(out=xn1_or[:, a, :], in_=xnt[:]), after=[xnk], tag=f"{A}sxn")
        xn_sts.append(stxn)
        sqh = f.op('scalar', lambda e: e.activation(sq[:], xnt[:], AF.Square, accum_out=stats[:, 2:3]),
                   after=[xnk], tag=f"{A}sqh")
        lnh = f.op('scalar', lambda e: e.activation(stats[:, 7:8], stats[:, 2:3], AF.Ln, scale=1.0 / 2048, bias=1e-5),
                   after=[sqh], tag=f"{A}lnh")
        exh = f.op('scalar', lambda e: e.activation(stats[:, 6:7], stats[:, 7:8], AF.Exp, scale=-0.5),
                   after=[lnh], tag=f"{A}exh")
        hk = f.op('vector', lambda e, a=a: e.tensor_scalar(hbf[:, a * 2048:(a + 1) * 2048], xnt[:],
                                                           stats[:, 6:7], None, ALU.mult),
                  after=[exh, t_og] + x1_sts[-1:], tag=f"{A}h1")
        hk2 = f.op('vector', lambda e, a=a: e.tensor_mul(hbf[:, a * 2048:(a + 1) * 2048],
                                                         hbf[:, a * 2048:(a + 1) * 2048], anwb[:]),
                   after=[hk, t_anb], tag=f"{A}h2")
        prev_h = hk2
        for dtt in range(16):
            trk = mm('pT', lambda e, a=a, dtt=dtt: e.transpose(
                pT[:], hbf[:, a * 2048 + dtt * 128: a * 2048 + (dtt + 1) * 128], ident[:]),
                [hk2, t_id], f"{A}tr{dtt}")
            evk = evac('pT', 'scalar', lambda e, a=a, dtt=dtt: e.copy(
                hT[:, dtt * 1024 + a * 128: dtt * 1024 + (a + 1) * 128], pT[:]), [trk], f"{A}te{dtt}")
            hT_toks.append(evk)
    hT_done = hT_toks[-1]

    t_ra = f.dma('sync', lambda e: e.dma_start(out=rA_sb[:].rearrange("p (a n) -> p a n", n=2048), in_=rA_r),
                 after=[x1_sts[-1]], tag="cra")
    t_rb = f.dma('sync', lambda e: e.dma_start(out=rB_sb[:].rearrange("p (a n) -> p a n", n=2048), in_=rB_r),
                 tag="crb")

    st_fin = []
    ci_ = 0
    prev_use = {}
    for name, wten, ncols, o_r, do_rope in (("q", w_q, 2048, q_or, True),
                                            ("k", w_k, 512, k_or, True),
                                            ("v", w_v, 512, v_or, False)):
        w_r = wten.rearrange("(a p) n -> p a n", p=128)
        for c0 in range(0, ncols, 512):
            key = (name, c0)
            ld = f.dma('gpsimd', lambda e, w_r=w_r, c0=c0, ci_=ci_: e.dma_start(
                out=qwb[ci_ % 2].rearrange("p (a n) -> p a n", n=512), in_=w_r[:, :, c0:c0 + 512]),
                after=[prev_use.get(ci_ - 2)], tag=f"ld{name}{c0}")
            for a in range(8):
                last = None
                for dtt in range(16):
                    last = mm('pA', lambda e, a=a, dtt=dtt, ci_=ci_: e.matmul(
                        pA[:, :], hT[:, dtt * 1024 + a * 128: dtt * 1024 + (a + 1) * 128],
                        qwb[ci_ % 2][:, dtt * 512:(dtt + 1) * 512],
                        start=(dtt == 0), stop=(dtt == 15)),
                        ([ld, hT_done] if dtt == 0 else []), f"p{name}{c0}_{a}_{dtt}")
                if do_rope:
                    roff = a * 2048 + c0 if name == "q" else a * 2048
                    r1 = evac('pA', 'vector', lambda e, roff=roff: e.tensor_tensor(
                        evq[:], pA[:, :], rA_sb[:, roff:roff + 512], ALU.mult),
                        [last, t_ra] + st_fin[-1:], f"r1{name}{c0}_{a}")
                    swap = bass.AP(pA, 1, [[512, 128], [2, 256], [-1, 2]])
                    r2 = f.op('vector', lambda e, roff=roff, swap=swap: e.tensor_tensor(
                        evq2[:].rearrange("p (i two) -> p i two", two=2), swap,
                        rB_sb[:, roff:roff + 512].rearrange("p (i two) -> p i two", two=2),
                        ALU.mult), after=[last, t_rb], tag=f"r2{name}{c0}_{a}")
                    bank['pA'] = r2
                    r3 = f.op('vector', lambda e: e.tensor_add(evq[:], evq[:], evq2[:]),
                              after=[r1, r2], tag=f"r3{name}{c0}_{a}")
                    fin = r3
                else:
                    fin = evac('pA', 'vector', lambda e: e.tensor_copy(evq[:], pA[:, :]),
                               [last] + st_fin[-1:], f"r1{name}_{a}")
                stq = f.dma('sync', lambda e, a=a, c0=c0, o_r=o_r, ncols=ncols: e.dma_start(
                    out=(o_r[:, a, c0:c0 + 512] if ncols > 512 else o_r[:, a, :]), in_=evq[:]),
                    after=[fin], tag=f"s{name}{c0}_{a}")
                st_fin.append(stq)
            prev_use[ci_] = st_fin[-1]
            ci_ += 1

    lw = f.dma('sync', lambda e: e.dma_start(out=wrow[:], in_=wabs8), tag="lwabs")
    sm1 = f.op('vector', lambda e: e.tensor_reduce(thr[0:1, 0:2].unsqueeze(2),
                                                   wrow[:].rearrange("o (e c) -> o e c", c=8),
                                                   AX.X, ALU.add), after=[lw], tag="wsum")
    thm = mm('pA', lambda e: e.matmul(pA[:, 0:2], ones1[:], thr[0:1, 0:2], start=True, stop=True),
             [sm1, t_ones] + [st_fin[-1]], "thmm")
    thv = evac('pA', 'scalar', lambda e: e.mul(thr[:, 0:2], pA[:, 0:2], 0.7 / (8192.0 * 2048.0)), [thm], "thev")

    def quant_abs(w_sl_fn, ntile, width, out_fn, col, start_dep):
        t = start_dep
        for i in range(ntile):
            ldt = f.dma('gpsimd', lambda e, i=i: e.dma_start(out=qw_f[:, :width], in_=w_sl_fn(i)),
                        after=[t], tag=f"qa{col}ld{i}")
            ab = f.op('scalar', lambda e: e.activation(qaux[:, :width], qw_f[:, :width], AF.Abs),
                      after=[ldt], tag=f"qa{col}ab{i}")
            ge = f.op('vector', lambda e: e.tensor_scalar(qaux[:, :width], qaux[:, :width], thr[:, col:col + 1],
                                                          None, ALU.is_ge), after=[ab, thv], tag=f"qa{col}ge{i}")
            sg = f.op('scalar', lambda e: e.sign(qw_f[:, :width], qw_f[:, :width]), after=[ge], tag=f"qa{col}sg{i}")
            mu = f.op('vector', lambda e: e.tensor_mul(qout[:, :width], qw_f[:, :width], qaux[:, :width]),
                      after=[sg], tag=f"qa{col}mu{i}")
            t = f.dma('sync', lambda e, i=i: e.dma_start(out=out_fn(i), in_=qout[:, :width]), after=[mu], tag=f"qa{col}st{i}")
        return t

    w1_r = w1_sl.rearrange("(a p) n -> p a n", p=128)
    w2_r = w2_sl.rearrange("(a p) (c w) -> p a c w", p=128, w=4096)
    w1q_or = w1q_sl.rearrange("(a p) n -> p a n", p=128)
    w2q_or = w2q_sl.rearrange("(a p) (c w) -> p a c w", p=128, w=4096)
    qt1 = quant_abs(lambda i: w1_r[:, i, :], 8, 2048, lambda i: w1q_or[:, i, :], 0, st_fin[-1])
    qt2 = quant_abs(lambda i: w2_r[:, i // 2, i % 2, :], 4, 4096, lambda i: w2q_or[:, i // 2, i % 2, :], 1, qt1)

    f.op('gpsimd', lambda e: e.engine_nop(), after=st_fin[-3:] + xn_sts[-1:] + [qt2, x1_sts[-1]], tag="fin")
    f.emit(blk)
    st.close()
    return nc


def make_inputs_l3(ins, l1outs, l2outs):
    og = np.zeros((2, 4096, 4096), dtype=BF)
    for r in range(8):
        o = l2outs[r]["og_out"]
        for j in range(4):
            hv = 4 * r + j
            og[:, :, 128 * hv:128 * (hv + 1)] = np.transpose(o[:, j], (0, 2, 1))
    ffn0 = np.zeros((2, 4096, 2048), dtype=np.float32)
    for r in range(8):
        b, ti = r // 4, tok_idx(r)
        ffn0[b, ti] = l2outs[r]["ffn0_fm"].T
    half = 64
    inv = 1.0 / (10000.0 ** (np.arange(half, dtype=np.float64) / half))
    ang = np.arange(4096, dtype=np.float64)[:, None] * inv[None, :]
    cos = np.cos(ang).astype(np.float32)
    sin = np.sin(ang).astype(np.float32)
    A1 = np.empty((4096, 128), np.float32); A1[:, 0::2] = cos; A1[:, 1::2] = cos
    B1 = np.empty((4096, 128), np.float32); B1[:, 0::2] = -sin; B1[:, 1::2] = sin
    Aq = np.tile(A1, (1, 16)).astype(BF)
    Bq = np.tile(B1, (1, 16)).astype(BF)
    consts = np.tile(np.array([[1e-5, 1.0, 1e-6, 1.1920929e-07, 0, 0, 0, 0]], np.float32), (128, 1))
    wabs8 = np.ascontiguousarray(
        np.stack([np.concatenate([l1outs[r]["wabs"].reshape(2) for r in range(8)]).reshape(8, 2).T]).reshape(1, 16))
    maps = []
    for r in range(8):
        b, ti = r // 4, tok_idx(r)
        maps.append(dict(
            x_rows=np.ascontiguousarray(ins["L0_attn_x"][b, ti]),
            ffn0_rows=np.ascontiguousarray(ffn0[b, ti]),
            og_sl=np.ascontiguousarray(og[b, ti].T),
            W_out=ins["L0_attn_W_out"],
            w_q=ins["L1_attn_w_q"], w_k=ins["L1_attn_w_k"], w_v=ins["L1_attn_w_v"],
            anw=ins["L1_attn_attn_norm_w"].reshape(1, 2048),
            ropeA=np.ascontiguousarray(Aq[ti]), ropeB=np.ascontiguousarray(Bq[ti]),
            idn=np.eye(128, dtype=BF), consts=consts,
            w1_sl=np.ascontiguousarray(ins["L1_ffn_w1"][1024 * r:1024 * (r + 1)]),
            w2_sl=np.ascontiguousarray(ins["L1_ffn_w2"][256 * r:256 * (r + 1)]),
            wabs8=wabs8,
        ))
    return maps


F32 = mybir.dt.float32
BF16 = mybir.dt.bfloat16
AF = mybir.ActivationFunctionType
ALU = mybir.AluOpType
AX = mybir.AxisListType
BF = ml_dtypes.bfloat16
ISQ = 0.08838834764831845


def tok_idx(r):
    i = r % 4
    return np.concatenate([np.arange(512 * i, 512 * (i + 1)),
                           np.arange(512 * (7 - i), 512 * (8 - i))])


def build_l4():
    nc = bass.Bass("TRN2", target_bir_lowering=False, debug=False)
    dt = nc.dram_tensor
    q_fm = dt("q_fm", [4, 128, 4096], BF16, kind="ExternalInput").ap()
    K_fm = dt("K_fm", [128, 4096], BF16, kind="ExternalInput").ap()
    V_tok = dt("V_tok", [4096, 128], BF16, kind="ExternalInput").ap()
    dmask = dt("dmask", [128, 4, 512], BF16, kind="ExternalInput").ap()
    onesd = dt("onesd", [128, 1], BF16, kind="ExternalInput").ap()

    ao_out = dt("ao_out", [4, 128, 4096], BF16, kind="ExternalOutput").ap()

    st = ExitStack()
    E = st.enter_context
    def S(name, shape, d=F32):
        return E(nc.sbuf_tensor(name, shape, d))
    blk = E(nc.Block())
    f = Flow(nc, st, "f_")

    qh_sb = S("qh_sb", [128, 4096], BF16)
    K_sb = S("K_sb", [128, 4096], BF16)
    V_sb = S("V_sb", [128, 32 * 128], BF16)
    dm_sb = S("dm_sb", [128, 4 * 512], BF16)
    onescb = S("onescb", [128, 1], BF16)
    ones1f = S("ones1f", [1, 128])
    pt_sb = S("pt_sb", [128, 512], BF16)
    den = S("den", [1, 512])
    deni = S("deni", [1, 512])
    aoev = S("aoev", [128, 512], BF16)
    shcol = S("shcol", [128, 1])
    rowt = S("rowt", [1, 4096])
    m1r = S("m1r", [1, 1])
    m2r = S("m2r", [1, 1])
    dben = S("dben", [128, 512])

    pS = E(nc.psum_tensor("pS", [128, 512], F32))
    pV = E(nc.psum_tensor("pV", [128, 512], F32))
    pD = E(nc.psum_tensor("pD", [1, 512], F32))
    pB = E(nc.psum_tensor("pB", [128, 512], F32))
    pR = E(nc.psum_tensor("pR", [1, 512], F32))

    bank = {}
    def mm(bname, emit, deps, tag):
        return f.op('tensor', emit, after=list(deps) + ([bank[bname]] if bname in bank else []), tag=tag)
    def evac(bname, eng, emit, deps, tag):
        t = f.op(eng, emit, after=list(deps), tag=tag)
        bank[bname] = t
        return t

    t_k = f.dma('sync', lambda e: e.dma_start(out=K_sb[:], in_=K_fm), tag="lk")
    t_v = f.dma('sync', lambda e: e.dma_start(out=V_sb[:].rearrange("p (a n) -> a p n" if False else "p (a n) -> p a n", n=128),
                                              in_=V_tok.rearrange("(a p) n -> p a n", p=128)), tag="lv")
    t_dm = f.dma('sync', lambda e: e.dma_start(out=dm_sb[:].rearrange("p (a n) -> p a n", n=512),
                                               in_=dmask), tag="ldm")
    t_o = f.dma('sync', lambda e: e.dma_start(out=onescb[:], in_=onesd), tag="lo")
    t_of = f.op('vector', lambda e: e.memset(ones1f[:], 1.0), tag="of")

    ksq = f.op('scalar', lambda e: e.activation(qh_sb[:], K_sb[:], AF.Square), after=[t_k], tag="ksq")
    kst = None
    for c0 in range(0, 4096, 512):
        m = mm('pR', lambda e, c0=c0: e.matmul(pR[:], onescb[:], qh_sb[:, c0:c0 + 512], start=True, stop=True),
               [ksq, t_o] + ([kst] if kst else []), f"knm{c0}")
        kst = evac('pR', 'vector', lambda e, c0=c0: e.tensor_copy(rowt[0:1, c0:c0 + 512], pR[:]), [m], f"kne{c0}")
    kmx = f.op('vector', lambda e: e.reduce_max(m1r[:], rowt[:], axis=AX.X), after=[kst], tag="kmx")

    ao_sts = []
    prev_q = None
    for qh in range(4):
        Q = f"q{qh}"
        lq = f.dma('sync', lambda e, qh=qh: e.dma_start(out=qh_sb[:], in_=q_fm[qh]),
                   after=[prev_q] if prev_q else [kmx], tag=f"{Q}lq")
        qst = None
        for c0 in range(0, 4096, 512):
            sqc = f.op('scalar', lambda e, c0=c0: e.activation(aoev[:], qh_sb[:, c0:c0 + 512], AF.Square),
                       after=[lq] + ([qst] if qst else []) + ao_sts[-1:], tag=f"{Q}sq{c0}")
            m = mm('pR', lambda e, c0=c0: e.matmul(pR[:], onescb[:], aoev[:],
                                                   start=True, stop=True), [sqc, t_o], f"{Q}qm{c0}")
            qst = evac('pR', 'vector', lambda e, c0=c0: e.tensor_copy(rowt[0:1, c0:c0 + 512], pR[:]), [m], f"{Q}qe{c0}")
        qmx = f.op('vector', lambda e: e.reduce_max(m2r[:], rowt[:], axis=AX.X), after=[qst], tag=f"{Q}qmx")
        sh1 = f.op('vector', lambda e: e.tensor_mul(m2r[:], m2r[:], m1r[:]), after=[qmx, kmx], tag=f"{Q}sh1")
        sh2 = f.op('scalar', lambda e: e.activation(m2r[:], m2r[:], AF.Sqrt), after=[sh1], tag=f"{Q}sh2")
        sh3 = f.op('vector', lambda e: e.tensor_scalar(m2r[:], m2r[:], -ISQ, None, ALU.mult), after=[sh2], tag=f"{Q}sh3")
        shm = mm('pB', lambda e: e.matmul(pB[:, 0:1], ones1f[:], m2r[:],
                                          start=True, stop=True), [sh3, t_of], f"{Q}shm")
        shv = evac('pB', 'vector', lambda e: e.tensor_copy(shcol[:], pB[:, 0:1]), [shm], f"{Q}shv")
        for s in range(8):
            SS = f"{Q}s{s}"
            nk = 4 * s + 4
            vdep = None
            dmm = None
            pt_toks = []
            for kj in range(nk):
                first = (kj == 0)
                last_kj = (kj == nk - 1)
                smm = mm('pS', lambda e, kj=kj, s=s: e.matmul(
                    pS[:, :], K_sb[:, 128 * kj:128 * (kj + 1)], qh_sb[:, 512 * s:512 * (s + 1)],
                    start=True, stop=True), [lq, t_k], f"{SS}sm{kj}")
                ex = evac('pS', 'scalar', lambda e: e.activation(pt_sb[:], pS[:, :], AF.Exp, scale=ISQ, bias=shcol[:]),
                          [smm, shv], f"{SS}ex{kj}")
                ptok = ex
                if kj >= 4 * s:
                    mrel = kj - 4 * s
                    ptok = f.op('vector', lambda e, mrel=mrel: e.tensor_mul(
                        pt_sb[:], pt_sb[:], dm_sb[:, 512 * mrel:512 * (mrel + 1)]), after=[ex, t_dm], tag=f"{SS}mk{kj}")
                vmm = mm('pV', lambda e, kj=kj, s=s, first=first, nk=nk: e.matmul(
                    pV[:, :], V_sb[:, 128 * kj:128 * (kj + 1)], pt_sb[:, :],
                    start=first, stop=(kj == nk - 1)), [ptok, t_v], f"{SS}vm{kj}")
                dmm = mm('pD', lambda e, first=first, kj=kj, nk=nk: e.matmul(
                    pD[:], onescb[:], pt_sb[:, :], start=first, stop=(kj == nk - 1)),
                    [ptok, t_o], f"{SS}dm{kj}")
                pt_toks = [vmm, dmm]
                bank['pS'] = dmm
            dev = evac('pD', 'vector', lambda e: e.tensor_copy(den[:], pD[:]), [dmm], f"{SS}de")
            dre = f.op('vector', lambda e: e.reciprocal(deni[:], den[:]), after=[dev], tag=f"{SS}dr")
            dbm = mm('pB', lambda e: e.matmul(pB[:, :], ones1f[:], deni[:],
                                              start=True, stop=True), [dre, t_of], f"{SS}db")
            dbe = evac('pB', 'vector', lambda e: e.tensor_copy(dben[:], pB[:, :]), [dbm], f"{SS}dbe")
            aoe = evac('pV', 'vector', lambda e: e.tensor_tensor(aoev[:], pV[:, :], dben[:], ALU.mult),
                       [pt_toks[0], dbe] + ao_sts[-1:], f"{SS}ao")
            stt = f.dma('sync', lambda e, qh=qh, s=s: e.dma_start(out=ao_out[qh][:, 512 * s:512 * (s + 1)],
                                                                  in_=aoev[:]), after=[aoe], tag=f"{SS}st")
            ao_sts.append(stt)
        prev_q = ao_sts[-1]

    f.op('gpsimd', lambda e: e.engine_nop(), after=ao_sts[-8:], tag="fin")
    f.emit(blk)
    st.close()
    return nc


def make_inputs_l4(l3outs):
    q = np.zeros((2, 4096, 2048), dtype=BF)
    k = np.zeros((2, 4096, 512), dtype=BF)
    v = np.zeros((2, 4096, 512), dtype=BF)
    for r in range(8):
        b, ti = r // 4, tok_idx(r)
        q[b, ti] = l3outs[r]["q_out"]
        k[b, ti] = l3outs[r]["k_out"]
        v[b, ti] = l3outs[r]["v_out"]
    dm = np.zeros((4, 128, 512), dtype=BF)
    for m in range(4):
        kj = np.arange(128)[:, None]
        qi = np.arange(512)[None, :]
        dm[m] = (128 * m + kj <= qi).astype(BF)
    onesd = np.ones((128, 1), dtype=BF)
    maps = []
    for r in range(8):
        b, hj = r // 4, r % 4
        qh = [4 * hj + t for t in range(4)]
        maps.append(dict(
            q_fm=np.ascontiguousarray(np.transpose(q[b].reshape(4096, 16, 128)[:, qh, :], (1, 2, 0))),
            K_fm=np.ascontiguousarray(k[b].reshape(4096, 4, 128)[:, hj, :].T),
            V_tok=np.ascontiguousarray(v[b].reshape(4096, 4, 128)[:, hj, :]),
            dmask=np.ascontiguousarray(np.transpose(dm, (1, 0, 2))), onesd=onesd,
        ))
    return maps


F32 = mybir.dt.float32
BF16 = mybir.dt.bfloat16
AF = mybir.ActivationFunctionType
ALU = mybir.AluOpType
AX = mybir.AxisListType
BF = ml_dtypes.bfloat16


def tok_idx(r):
    i = r % 4
    return np.concatenate([np.arange(512 * i, 512 * (i + 1)),
                           np.arange(512 * (7 - i), 512 * (8 - i))])


def build_l5():
    nc = bass.Bass("TRN2", target_bir_lowering=False, debug=False)
    dt = nc.dram_tensor
    ao_sl = dt("ao_sl", [2048, 1024], BF16, kind="ExternalInput").ap()
    wo_t = dt("wo_t", [16, 128, 16, 128], F32, kind="ExternalInput").ap()
    x1_fm = dt("x1_fm", [2048, 1024], F32, kind="ExternalInput").ap()
    xn1_fm = dt("xn1_fm", [2048, 1024], BF16, kind="ExternalInput").ap()
    mnw_cols = dt("mnw_cols", [128, 16], F32, kind="ExternalInput").ap()
    gate_w = dt("gate_w", [2048, 8], F32, kind="ExternalInput").ap()
    w1t = dt("w1t", [8, 8, 128, 16, 128], F32, kind="ExternalInput").ap()
    w3t = dt("w3t", [8, 8, 128, 16, 128], F32, kind="ExternalInput").ap()
    w2t = dt("w2t", [8, 16, 128, 8, 128], F32, kind="ExternalInput").ap()
    w1q_t = dt("w1q_t", [64, 128, 16, 128], BF16, kind="ExternalInput").ap()
    w2q_t = dt("w2q_t", [64, 128, 16, 128], BF16, kind="ExternalInput").ap()
    b1c = dt("b1c", [128, 64], F32, kind="ExternalInput").ap()
    b2c = dt("b2c", [128, 16], F32, kind="ExternalInput").ap()
    idnf = dt("idnf", [128, 128], F32, kind="ExternalInput").ap()
    consts = dt("consts", [128, 8], F32, kind="ExternalInput").ap()

    x2_fm = dt("x2_fm", [2048, 1024], F32, kind="ExternalOutput").ap()
    h_scr = dt("h_scr", [64, 128, 1024], BF16, kind="Internal").ap()
    c_scr = dt("c_scr", [8, 1024], F32, kind="Internal").ap()

    st = ExitStack()
    E = st.enter_context
    def S(name, shape, d=F32):
        return E(nc.sbuf_tensor(name, shape, d))
    blk = E(nc.Block())
    f = Flow(nc, st, "f_")

    ao_sb = S("ao_sb", [128, 16 * 1024], BF16)
    xnf = S("xnf", [128, 16 * 1024], BF16)
    xa = S("xa", [128, 16 * 1024], BF16)
    hm = S("hm", [128, 16 * 1024], BF16)
    he = S("he", [128, 8 * 1024], BF16)
    gsc = S("gsc", [128, 1024], BF16)
    wst = [S(f"wst{i}", [128, 2048], BF16) for i in range(2)]
    wst2 = [S(f"wst2{i}", [128, 2048], BF16) for i in range(2)]
    xtmp = S("xtmp", [128, 2048])
    coeff = S("coeff", [128, 64])
    crow = S("crow", [8, 1024])
    gwb = S("gwb", [128, 128], BF16)
    cre = S("cre", [1, 1024])
    cb = S("cb", [128, 1024], BF16)
    stats = S("stats", [128, 8])
    mnw = S("mnw", [128, 16])
    b1s = S("b1s", [128, 64])
    b2s = S("b2s", [128, 16])
    onesb = S("onesb", [1, 128], BF16)
    onescb = S("onescb", [128, 1], BF16)
    ones1f = S("ones1f", [1, 128])
    idf = S("idf", [128, 128])
    cons = S("cons", [128, 8])
    evb = S("evb", [128, 512], BF16)
    rrow = S("rrow", [1, 1024])
    acc1 = ao_sb[:].bitcast(F32)
    acc2 = xnf[:].bitcast(F32)

    def acc_sl(dte, c0, w=512):
        base = dte * 1024 + c0
        return acc1[:, base:base + w] if dte < 8 else acc2[:, base - 8192:base - 8192 + w]

    pK = [E(nc.psum_tensor(f"pK{i}", [128, 512], F32)) for i in range(8)]

    bank = {}
    def mm(bi, emit, deps, tag):
        return f.op('tensor', emit, after=list(deps) + ([bank[bi]] if bi in bank else []), tag=tag)
    def evac(bi, eng, emit, deps, tag):
        t = f.op(eng, emit, after=list(deps), tag=tag)
        bank[bi] = t
        return t

    nc.const_aps.register(F32, 1e-5, cons[:, 0:1])

    ao_r = ao_sl.rearrange("(a p) n -> p a n", p=128)
    xnf_r = xn1_fm.rearrange("(a p) n -> p a n", p=128)
    x1_r = x1_fm.rearrange("(a p) n -> p a n", p=128)
    gw_r = gate_w.rearrange("(a p) n -> p a n", p=128)
    x2_or = x2_fm.rearrange("(a p) n -> p a n", p=128)

    t_ao = f.dma('gpsimd', lambda e: e.dma_start(out=ao_sb[:].rearrange("p (a n) -> p a n", n=1024), in_=ao_r), tag="lao")
    t_xf = f.dma('sync', lambda e: e.dma_start(out=xnf[:].rearrange("p (a n) -> p a n", n=1024), in_=xnf_r), tag="lxf")
    t_mn = f.dma('sync', lambda e: e.dma_start(out=mnw[:], in_=mnw_cols), tag="lmn")
    t_b1 = f.dma('sync', lambda e: e.dma_start(out=b1s[:], in_=b1c), tag="lb1")
    t_b2 = f.dma('sync', lambda e: e.dma_start(out=b2s[:], in_=b2c), tag="lb2")
    t_cc = f.dma('sync', lambda e: e.dma_start(out=cons[:], in_=consts), tag="lcc")
    t_if = f.dma('sync', lambda e: e.dma_start(out=idf[:], in_=idnf), tag="lif")
    t_ob = f.op('vector', lambda e: e.memset(onesb[:], 1.0), tag="ob")
    t_oc = f.op('vector', lambda e: e.memset(onescb[:], 1.0), tag="oc")
    t_of = f.op('vector', lambda e: e.memset(ones1f[:], 1.0), tag="of")

    xa_toks = []
    wl = {}
    def emit_wo(dte, deps):
        wl[dte] = f.dma('gpsimd', lambda e, dte=dte: e.dma_start(
            out=wst[dte % 2][:].rearrange("p (a n) -> p a n", n=128), in_=wo_t[dte]), after=deps, tag=f"wo{dte}")
    emit_wo(0, [])
    emit_wo(1, [])
    for dte in range(16):
        for tg in range(2):
            bi = (dte * 2 + tg) % 3
            last = None
            for dvt in range(16):
                last = mm(bi, lambda e, dte=dte, tg=tg, dvt=dvt, bi=bi: e.matmul(
                    pK[bi][:, :], wst[dte % 2][:, 128 * dvt:128 * (dvt + 1)],
                    ao_sb[:, dvt * 1024 + 512 * tg: dvt * 1024 + 512 * (tg + 1)],
                    start=(dvt == 0), stop=(dvt == 15)),
                    [wl[dte], t_ao] if dvt == 0 else [], f"P0m{dte}_{tg}_{dvt}")
            ev = evac(bi, 'vector', lambda e, dte=dte, tg=tg, bi=bi: e.tensor_add(
                xa[:, dte * 1024 + 512 * tg: dte * 1024 + 512 * (tg + 1)],
                xnf[:, dte * 1024 + 512 * tg: dte * 1024 + 512 * (tg + 1)], pK[bi][:, :]),
                [last, t_xf], f"P0e{dte}_{tg}")
            xa_toks.append(ev)
            last_rd0 = last
        if dte + 2 < 16:
            emit_wo(dte + 2, [last_rd0])
    xa_done = xa_toks[-1]

    sq_toks = []
    last_s = None
    for tg in range(2):
        for dte in range(16):
            sqt = f.op('scalar', lambda e, dte=dte, tg=tg: e.activation(
                evb[:], xa[:, dte * 1024 + 512 * tg: dte * 1024 + 512 * (tg + 1)], AF.Square),
                after=[xa_done] + ([last_s] if last_s else []), tag=f"P1s{tg}_{dte}")
            m = mm(4, lambda e, dte=dte: e.matmul(pK[4][:1, :], onescb[:], evb[:],
                                                  start=(dte == 0), stop=(dte == 15)),
                   [sqt, t_oc], f"P1m{tg}_{dte}")
            last_s = m
        ev = evac(4, 'vector', lambda e, tg=tg: e.tensor_copy(rrow[0:1, 512 * tg:512 * (tg + 1)], pK[4][:1, :]),
                  [last_s], f"P1e{tg}")
        last_s = ev
        sq_toks.append(ev)
    rl = f.op('scalar', lambda e: e.activation(rrow[:], rrow[:], AF.Ln, scale=1.0 / 2048, bias=1e-5),
              after=[sq_toks[-1], t_cc], tag="P1ln")
    rx = f.op('scalar', lambda e: e.activation(rrow[:], rrow[:], AF.Exp, scale=-0.5), after=[rl], tag="P1ex")
    rb_toks = []
    for tg in range(2):
        m = mm(4, lambda e, tg=tg: e.matmul(pK[4][:, :], ones1f[:], rrow[0:1, 512 * tg:512 * (tg + 1)],
                                            start=True, stop=True), [rx, t_of], f"P1b{tg}")
        ev = evac(4, 'scalar', lambda e, tg=tg: e.copy(gsc[:, 512 * tg:512 * (tg + 1)], pK[4][:, :]), [m], f"P1be{tg}")
        rb_toks.append(ev)
    hm_toks = []
    for dte in range(16):
        t = f.op('vector', lambda e, dte=dte: e.scalar_tensor_tensor(
            hm[:, dte * 1024:(dte + 1) * 1024], xa[:, dte * 1024:(dte + 1) * 1024],
            mnw[:, dte:dte + 1], gsc[:, 0:1024], ALU.mult, ALU.mult),
            after=[rb_toks[-1], t_mn, xa_done], tag=f"P1h{dte}")
        hm_toks.append(t)
    hm_ready = hm_toks[-1]

    gwl = f.dma('gpsimd', lambda e: e.dma_start(out=gwb[:].rearrange("p (a n) -> p a n", n=8),
                                                in_=gw_r), after=[], tag="lgw")
    co_toks = []
    tr_toks = []
    for a in range(8):
        A = f"P2a{a}"
        last = None
        for dtt in range(16):
            last = mm(5, lambda e, a=a, dtt=dtt: e.matmul(
                pK[5][:, 0:8], hm[:, dtt * 1024 + a * 128: dtt * 1024 + (a + 1) * 128],
                gwb[:, 8 * dtt: 8 * (dtt + 1)],
                start=(dtt == 0), stop=(dtt == 15)), [gwl, hm_ready] if dtt == 0 else [], f"{A}g{dtt}")
        mx = evac(5, 'vector', lambda e: e.reduce_max(stats[:, 0:1], pK[5][:, 0:8], axis=AX.X,
                                                      negate=True), [last], f"{A}mx")
        exp8 = f.op('scalar', lambda e, a=a: e.activation(coeff[:, 8 * a:8 * (a + 1)], pK[5][:, 0:8], AF.Exp,
                                                          bias=stats[:, 0:1], scale=1.0), after=[mx], tag=f"{A}e8")
        bank[5] = exp8
        sm = f.op('vector', lambda e, a=a: e.reduce_sum(stats[:, 1:2], coeff[:, 8 * a:8 * (a + 1)], axis=AX.X),
                  after=[exp8], tag=f"{A}sm")
        rc = f.op('vector', lambda e: e.reciprocal(stats[:, 2:3], stats[:, 1:2]), after=[sm], tag=f"{A}rc")
        pr = f.op('vector', lambda e, a=a: e.tensor_scalar(coeff[:, 8 * a:8 * (a + 1)], coeff[:, 8 * a:8 * (a + 1)],
                                                           stats[:, 2:3], None, ALU.mult), after=[rc], tag=f"{A}pr")
        m1 = f.op('vector', lambda e, a=a: e.reduce_max(stats[:, 3:4], coeff[:, 8 * a:8 * (a + 1)], axis=AX.X),
                  after=[pr], tag=f"{A}m1")
        lt = f.op('vector', lambda e, a=a: e.tensor_scalar(xtmp[:, 128:136], coeff[:, 8 * a:8 * (a + 1)],
                                                           stats[:, 3:4], None, ALU.is_lt), after=[m1], tag=f"{A}lt")
        p2 = f.op('vector', lambda e, a=a: e.tensor_mul(xtmp[:, 136:144], coeff[:, 8 * a:8 * (a + 1)],
                                                        xtmp[:, 128:136]), after=[lt], tag=f"{A}p2")
        m2 = f.op('vector', lambda e: e.reduce_max(stats[:, 4:5], xtmp[:, 136:144], axis=AX.X), after=[p2], tag=f"{A}m2")
        ge2 = f.op('vector', lambda e, a=a: e.tensor_scalar(xtmp[:, 128:136], coeff[:, 8 * a:8 * (a + 1)],
                                                            stats[:, 4:5], None, ALU.is_ge), after=[m2], tag=f"{A}ge")
        s12 = f.op('vector', lambda e: e.tensor_add(stats[:, 5:6], stats[:, 3:4], stats[:, 4:5]), after=[m2], tag=f"{A}s12")
        rc2 = f.op('vector', lambda e: e.reciprocal(stats[:, 6:7], stats[:, 5:6]), after=[s12], tag=f"{A}rc2")
        cf = f.op('vector', lambda e, a=a: e.tensor_mul(coeff[:, 8 * a:8 * (a + 1)], coeff[:, 8 * a:8 * (a + 1)],
                                                        xtmp[:, 128:136]), after=[ge2], tag=f"{A}cf")
        cf2 = f.op('vector', lambda e, a=a: e.tensor_scalar(coeff[:, 8 * a:8 * (a + 1)], coeff[:, 8 * a:8 * (a + 1)],
                                                            stats[:, 6:7], None, ALU.mult), after=[cf, rc2], tag=f"{A}cf2")
        co_toks.append(cf2)
        trc = mm(6, lambda e, a=a: e.transpose(pK[6][0:8, 0:128], coeff[:, 8 * a:8 * (a + 1)], idf[:]),
                 [cf2, t_if], f"{A}tc")
        evc = evac(6, 'vector', lambda e, a=a: e.tensor_copy(crow[0:8, 128 * a:128 * (a + 1)], pK[6][0:8, 0:128]),
                   [trc], f"{A}ec")
        tr_toks.append(evc)
    stc = f.dma('sync', lambda e: e.dma_start(out=c_scr, in_=crow[:]), after=[tr_toks[-1]], tag="P2st")

    h_sts = []
    wl1 = {}
    def emit_w1(ht, deps):
        wl1[ht] = f.dma('sync', lambda e, ht=ht: e.dma_start(
            out=wst2[ht % 2][:].rearrange("p (a n) -> p a n", n=128), in_=w1q_t[ht]), after=deps, tag=f"w1{ht}")
    emit_w1(0, [])
    emit_w1(1, [])
    prev_g = {}
    for ht in range(64):
        HT = f"P3h{ht}"
        for tg in range(2):
            bi = tg
            last = None
            for dtt in range(16):
                last = mm(bi, lambda e, ht=ht, tg=tg, dtt=dtt, bi=bi: e.matmul(
                    pK[bi][:, :], wst2[ht % 2][:, 128 * dtt:128 * (dtt + 1)],
                    xnf[:, dtt * 1024 + 512 * tg: dtt * 1024 + 512 * (tg + 1)],
                    start=(dtt == 0), stop=(dtt == 15)),
                    [wl1[ht], t_xf] if dtt == 0 else [], f"{HT}m{tg}_{dtt}")
            gl = evac(bi, 'scalar', lambda e, ht=ht, bi=bi: e.activation(evb[:], pK[bi][:, :], AF.Gelu,
                                                                         bias=b1s[:, ht:ht + 1]),
                      [last, t_b1] + h_sts[-2:] + sq_toks[-1:], f"{HT}gl{tg}")
            stt = f.dma('sync', lambda e, ht=ht, tg=tg: e.dma_start(
                out=h_scr[ht, :, 512 * tg:512 * (tg + 1)], in_=evb[:]), after=[gl], tag=f"{HT}st{tg}")
            h_sts.append(stt)
            last_rd1 = last
        if ht + 2 < 64:
            emit_w1(ht + 2, [last_rd1])
    h_done = h_sts[-1]

    ai_toks = []
    for dte in range(16):
        lx1 = f.dma('sync', lambda e, dte=dte: e.dma_start(out=xtmp[:, 0:1024], in_=x1_r[:, dte, :]),
                    after=[h_done] + ai_toks[-1:], tag=f"P4l{dte}")
        t = f.op('vector', lambda e, dte=dte: e.tensor_add(
            acc_sl(dte, 0, 1024), xtmp[:, 0:1024], xa[:, dte * 1024:(dte + 1) * 1024]),
            after=[lx1, hm_ready, h_done], tag=f"P4a{dte}")
        ai_toks.append(t)
    acc_ready = ai_toks[-1]

    moe_last = acc_ready
    prev_w = {}
    rdw = {0: acc_ready, 1: acc_ready}
    for e_ in range(8):
        EE = f"P5e{e_}"
        lcr = f.dma('sync', lambda e, e_=e_: e.dma_start(out=cre[:], in_=c_scr[e_].rearrange("(o n) -> o n", o=1)),
                    after=[stc] + ([prev_w.get(e_ - 1)] if e_ > 0 else []), tag=f"{EE}lc")
        cbm = []
        for tg in range(2):
            m = mm(6, lambda e, tg=tg: e.matmul(pK[6][:, :], ones1f[:], cre[0:1, 512 * tg:512 * (tg + 1)],
                                                start=True, stop=True), [lcr, t_of], f"{EE}cb{tg}")
            ev = evac(6, 'scalar', lambda e, tg=tg: e.copy(cb[:, 512 * tg:512 * (tg + 1)], pK[6][:, :]), [m], f"{EE}ce{tg}")
            cbm.append(ev)
        he_toks = []
        for it in range(8):
            l1_ = f.dma('gpsimd', lambda e, e_=e_, it=it: e.dma_start(
                out=wst[0][:].rearrange("p (a n) -> p a n", n=128), in_=w1t[e_, it]),
                after=[rdw[0]], tag=f"{EE}w1_{it}")
            l3_ = f.dma('gpsimd', lambda e, e_=e_, it=it: e.dma_start(
                out=wst[1][:].rearrange("p (a n) -> p a n", n=128), in_=w3t[e_, it]),
                after=[rdw[1]], tag=f"{EE}w3_{it}")
            for tg in range(2):
                lg = None
                lu = None
                for dtt in range(16):
                    lg = mm(0, lambda e, it=it, tg=tg, dtt=dtt: e.matmul(
                        pK[0][:, :], wst[0][:, 128 * dtt:128 * (dtt + 1)],
                        hm[:, dtt * 1024 + 512 * tg: dtt * 1024 + 512 * (tg + 1)],
                        start=(dtt == 0), stop=(dtt == 15)), [l1_, hm_ready] if dtt == 0 else [], f"{EE}g{it}_{tg}_{dtt}")
                for dtt in range(16):
                    lu = mm(1, lambda e, it=it, tg=tg, dtt=dtt: e.matmul(
                        pK[1][:, :], wst[1][:, 128 * dtt:128 * (dtt + 1)],
                        hm[:, dtt * 1024 + 512 * tg: dtt * 1024 + 512 * (tg + 1)],
                        start=(dtt == 0), stop=(dtt == 15)), [l3_, hm_ready] if dtt == 0 else [], f"{EE}u{it}_{tg}_{dtt}")
                sil = evac(0, 'scalar', lambda e: e.activation(gsc[:, 0:512], pK[0][:, :], AF.Silu),
                           [lg] + he_toks[-1:], f"{EE}s{it}_{tg}")
                mu = evac(1, 'vector', lambda e, it=it, tg=tg: e.tensor_tensor(
                    evb[:], gsc[:, 0:512], pK[1][:, :], ALU.mult), [sil, lu], f"{EE}m{it}_{tg}")
                fc_ = f.op('vector', lambda e, it=it, tg=tg: e.tensor_mul(
                    he[:, it * 1024 + 512 * tg: it * 1024 + 512 * (tg + 1)], evb[:],
                    cb[:, 512 * tg:512 * (tg + 1)]), after=[mu, cbm[-1]], tag=f"{EE}f{it}_{tg}")
                he_toks.append(fc_)
                rdw[0] = lg
                rdw[1] = lu
        he_done = he_toks[-1]
        wl2 = {}
        def emit_w2(dte, deps, e_=e_):
            wl2[dte] = f.dma('gpsimd', lambda e, e_=e_, dte=dte: e.dma_start(
                out=wst2[dte % 2][:, 0:1024].rearrange("p (a n) -> p a n", n=128), in_=w2t[e_, dte]),
                after=deps, tag=f"{EE}w2_{dte}")
        emit_w2(0, [he_done] + ([prev_w[(e_ - 1, 'w2', 14)]] if e_ > 0 else []))
        emit_w2(1, [he_done] + ([prev_w[(e_ - 1, 'w2', 15)]] if e_ > 0 else []))
        for dte in range(16):
            for tg in range(2):
                bi = 2 + ((dte * 2 + tg) % 3)
                last = None
                for it in range(8):
                    last = mm(bi, lambda e, dte=dte, tg=tg, it=it, bi=bi: e.matmul(
                        pK[bi][:, :], wst2[dte % 2][:, 128 * it:128 * (it + 1)],
                        he[:, it * 1024 + 512 * tg: it * 1024 + 512 * (tg + 1)],
                        start=(it == 0), stop=(it == 7)),
                        [wl2[dte], he_done] if it == 0 else [], f"{EE}o{dte}_{tg}_{it}")
                stp = evac(bi, 'vector', lambda e, dte=dte, tg=tg, bi=bi: e.tensor_add(
                    acc_sl(dte, 512 * tg), acc_sl(dte, 512 * tg), pK[bi][:, :]),
                    [last, acc_ready], f"{EE}a{dte}_{tg}")
                moe_last = stp
                prev_w[(e_, 'w2', dte)] = stp
                last_rd2 = last
            if dte + 2 < 16:
                emit_w2(dte + 2, [last_rd2])
        prev_w[e_] = moe_last

    out_sts = []
    fin_toks = []
    wlh = {}
    wlw = {}
    for blkd in range(4):
        for ht in range(64):
            key = (blkd, ht)
            war = [wlh.get((blkd, ht - 2), None)] if ht >= 2 else ([fin_toks[-1]] if fin_toks else [moe_last])
            war = [w for w in war if w is not None]
            lh = f.dma('sync', lambda e, ht=ht: e.dma_start(out=wst[ht % 2][:, 0:1024], in_=h_scr[ht]),
                       after=war + [h_done, moe_last], tag=f"P6h{blkd}_{ht}")
            lw = f.dma('gpsimd', lambda e, ht=ht, blkd=blkd: e.dma_start(
                out=wst2[ht % 2][:, 0:512].rearrange("p (a n) -> p a n", n=128),
                in_=w2q_t[ht][:, 4 * blkd:4 * (blkd + 1), :]), after=war, tag=f"P6w{blkd}_{ht}")
            for sd in range(4):
                dte = blkd * 4 + sd
                for tg in range(2):
                    bi = sd * 2 + tg
                    mmt = mm(bi, lambda e, ht=ht, sd=sd, tg=tg, bi=bi: e.matmul(
                        pK[bi][:, :], wst2[ht % 2][:, 128 * sd:128 * (sd + 1)],
                        wst[ht % 2][:, 512 * tg:512 * (tg + 1)],
                        start=(ht == 0), stop=(ht == 63)), [lh, lw], f"P6m{blkd}_{ht}_{sd}_{tg}")
                    wlh[(blkd, ht, sd, tg)] = mmt
            wlh[(blkd, ht)] = wlh[(blkd, ht, 3, 1)]
        for sd in range(4):
            dte = blkd * 4 + sd
            for tg in range(2):
                bi = sd * 2 + tg
                av = evac(bi, 'vector', lambda e, dte=dte, tg=tg, bi=bi: e.scalar_tensor_tensor(
                    acc_sl(dte, 512 * tg), pK[bi][:, :], b2s[:, dte:dte + 1],
                    acc_sl(dte, 512 * tg), ALU.add, ALU.add),
                    [wlh[(blkd, 63, sd, tg)], t_b2, moe_last], f"P6a{blkd}_{sd}_{tg}")
                stt = f.dma('sync', lambda e, dte=dte, tg=tg: e.dma_start(
                    out=x2_or[:, dte, 512 * tg:512 * (tg + 1)], in_=acc_sl(dte, 512 * tg)), after=[av],
                    tag=f"P6s{blkd}_{sd}_{tg}")
                out_sts.append(stt)
                fin_toks.append(stt)

    f.op('gpsimd', lambda e: e.engine_nop(), after=out_sts[-8:], tag="fin")
    f.emit(blk)
    st.close()
    return nc


def make_inputs_l5(ins, l3outs, l4outs):
    ao = np.zeros((2, 2048, 4096), dtype=BF)
    for r in range(8):
        b, hj = r // 4, r % 4
        for t in range(4):
            qh = 4 * hj + t
            ao[b, 128 * qh:128 * (qh + 1), :] = l4outs[r]["ao_out"][t]
    x1 = np.zeros((2, 4096, 2048), np.float32)
    xn1 = np.zeros((2, 4096, 2048), np.float32)
    for r in range(8):
        b, ti = r // 4, tok_idx(r)
        x1[b, ti] = l3outs[r]["x1_out"]
        xn1[b, ti] = l3outs[r]["xn1_out"]
    w1q = np.concatenate([l3outs[r]["w1q_sl"] for r in range(8)], axis=0)
    w2q = np.concatenate([l3outs[r]["w2q_sl"] for r in range(8)], axis=0)
    w1qT = np.ascontiguousarray(w1q.T)
    w2qT = np.ascontiguousarray(w2q.T)
    w1q_t = np.ascontiguousarray(np.transpose(w1qT.reshape(16, 128, 64, 128), (2, 1, 0, 3)))
    w2q_t = np.ascontiguousarray(np.transpose(w2qT.reshape(64, 128, 16, 128), (0, 1, 2, 3)))
    wo = ins["L1_attn_w_o"]
    wo_t = np.ascontiguousarray(np.transpose(wo.reshape(16, 128, 16, 128), (2, 1, 0, 3)))
    w1m = ins["L1_attn_w1"]; w3m = ins["L1_attn_w3"]; w2m = ins["L1_attn_w2"]
    w1t = np.ascontiguousarray(np.transpose(w1m.reshape(8, 16, 128, 8, 128), (0, 3, 2, 1, 4)))
    w3t = np.ascontiguousarray(np.transpose(w3m.reshape(8, 16, 128, 8, 128), (0, 3, 2, 1, 4)))
    w2t = np.ascontiguousarray(np.transpose(w2m.reshape(8, 8, 128, 16, 128), (0, 3, 2, 1, 4)))
    mnw = ins["L1_attn_moe_norm_w"].reshape(16, 128).T.copy()
    b1 = ins["L1_ffn_b1"].reshape(64, 128).T.copy()
    consts = np.tile(np.array([[1e-5, 1.0, 1e-6, 1.1920929e-07, 0, 0, 0, 0]], np.float32), (128, 1))
    maps = []
    for r in range(8):
        b, ti = r // 4, tok_idx(r)
        maps.append(dict(
            ao_sl=np.ascontiguousarray(ao[b][:, ti]),
            wo_t=wo_t,
            x1_fm=np.ascontiguousarray(x1[b, ti].T),
            xn1_fm=np.ascontiguousarray(xn1[b, ti].T).astype(BF),
            mnw_cols=mnw, gate_w=ins["L1_attn_gate_w"],
            w1t=w1t, w3t=w3t, w2t=w2t,
            w1q_t=w1q_t, w2q_t=w2q_t,
            b1c=b1, b2c=ins["L1_ffn_b2"].reshape(16, 128).T.copy(),
            idnf=np.eye(128, dtype=np.float32), consts=consts,
        ))
    return maps


def kernel(**inputs):
    ins = inputs
    cores = list(range(8))
    r1 = run_bass_kernel_spmd(build_l1(), make_inputs_l1(ins), core_ids=cores).results
    r2 = run_bass_kernel_spmd(build_l2(), make_inputs_l2(ins, r1), core_ids=cores).results
    r3 = run_bass_kernel_spmd(build_l3(), make_inputs_l3(ins, r1, r2), core_ids=cores).results
    r4 = run_bass_kernel_spmd(build_l4(), make_inputs_l4(r3), core_ids=cores).results
    r5 = run_bass_kernel_spmd(build_l5(), make_inputs_l5(ins, r3, r4), core_ids=cores).results
    out = np.zeros((2, 4096, 2048), np.float32)
    for r in range(8):
        b, ti = r // 4, tok_idx(r)
        out[b, ti] = r5[r]["x2_fm"].T
    return out
